# version 12 (final)
# baseline (speedup 1.0000x reference)
import contextlib
import numpy as np

import concourse.bass as bass
import concourse.mybir as mybir
from concourse.bass_utils import run_bass_kernel_spmd

F32 = mybir.dt.float32
BF16 = mybir.dt.bfloat16
AF = mybir.ActivationFunctionType
ALU = mybir.AluOpType
AXX = mybir.AxisListType.X

NCORES = 8
BPC = 2
T = 2048
C = 1024
NCH = 8
NH = 16
WIN = 512
HID = 2730
HIDP = 2816
NJ = 22
NEG = -1.0e30


_RUN = dict(mod=True, modcols=True, aw=True, batches=2, loads=True, p1=True, p3=True, p4=True, layers=2, attn=True, mlp=True, final=True)

ENGINES = ("tensor", "vector", "scalar", "gpsimd", "sync")


class _Ctr:

    def __init__(self, sem, name):
        self.sem, self.n, self.name = sem, 0, name

    def at(self, value=None):
        return (self.sem, self.n if value is None else value)


class _DmaCtr:

    def __init__(self, sems, name):
        self.sems = sems
        self.counts = [0] * len(sems)
        self.rr = 0
        self.name = name

    def issue(self):
        i = self.rr % len(self.sems)
        self.rr += 1
        selfwait = (self.sems[i], self.counts[i])
        self.counts[i] += 16
        return selfwait, (self.sems[i], 16), (self.sems[i], self.counts[i])

    def at(self):
        return [(s, c) for s, c in zip(self.sems, self.counts) if c > 0]


def _expand(w, out):
    if isinstance(w, list):
        for x in w:
            _expand(x, out)
    else:
        out.append(w)


class _Prog:
    def __init__(self, nc, es):
        self.nc, self.es = nc, es
        self.streams = {e: [] for e in ENGINES}
        self._lw = {}
        self._dma_lw = {}
        self.auto = {}

    def ctr(self, name):
        sem = self.es.enter_context(self.nc.semaphore(name))
        return _Ctr(sem, name)

    def dctr(self, name, k):
        sems = [self.es.enter_context(self.nc.semaphore(f"{name}{i}"))
                for i in range(k)]
        return _DmaCtr(sems, name)

    def emit(self, eng, fn, wait=(), inc=(), raw_incs=(), nodedup=False):
        flat = []
        _expand(list(wait), flat)
        waits = []
        dedup = not nodedup
        for item in flat:
            sem, value = item
            if isinstance(sem, _Ctr):
                sem = sem.sem
            if value > 0 and (not dedup or self._lw.get((eng, id(sem)), -1) < value):
                if (sem, value) not in waits:
                    waits.append((sem, value))
                self._lw[(eng, id(sem))] = value
        incs = list(raw_incs)
        for c in inc:
            c.n += 1
            incs.append((c.sem, 1))
        self.streams[eng].append((waits, fn, incs))

    def op(self, eng, method, *args, wait=(), inc=(), lowp=False, **kwargs):
        ac = self.auto.get(eng)
        if ac is not None:
            wait = list(wait) + [ac.at()]
            if ac not in inc:
                inc = list(inc) + [ac]
        if lowp:
            def fn(e, m=method, a=args, k=kwargs, nc=self.nc):
                with nc.allow_low_precision(reason="bf16 stat tables"):
                    return getattr(e, m)(*a, **k)
        else:
            fn = lambda e, m=method, a=args, k=kwargs: getattr(e, m)(*a, **k)
        self.emit(eng, fn, wait=wait, inc=inc)

    def dma(self, eng, out_ap, in_ap, ctr, wait=(), **kw):
        selfwait, incspec, done = ctr.issue()
        flat = []
        _expand(list(wait), flat)
        key = id(selfwait[0])
        seen = self._dma_lw.setdefault(key, {})
        out = []
        for sem, value in flat:
            if isinstance(sem, _Ctr):
                sem = sem.sem
            if value > 0 and seen.get(id(sem), -1) < value:
                out.append((sem, value))
                seen[id(sem)] = value
        self.emit(eng, lambda e, o=out_ap, i=in_ap, k=kw: e.dma_start(o, i, **k),
                  wait=out + [selfwait], raw_incs=[incspec], nodedup=True)
        return [done]

    def bar(self, eng, wait=(), inc=()):
        ac = self.auto.get(eng)
        if ac is not None:
            wait = list(wait) + [ac.at()]
            if ac not in inc:
                inc = list(inc) + [ac]
        self.emit(eng, lambda e: e.drain(), wait=wait, inc=inc)

    def replay(self, block):
        for eng_name in ENGINES:
            stream = self.streams[eng_name]
            if not stream:
                continue

            def body(e, stream=stream):
                for waits, fn, incs in stream:
                    for sem, value in waits[:-1]:
                        e.wait_ge(sem, value)
                    instr = fn(e)
                    if waits:
                        instr._wait_ge(waits[-1][0], waits[-1][1])
                    for sem, k in incs:
                        instr.then_inc(sem, k)

            getattr(block, eng_name)(body)


def _rot_tables():
    d = np.arange(64)
    j = d % 32
    sign = np.where(d % 64 < 32, -1.0, 1.0)
    half_idx = (2 * j).astype(np.float64)
    inv_freq = (np.float32(1.0) / np.float32(10000.0) ** (half_idx.astype(np.float32) / np.float32(64.0))).astype(np.float64)
    t = np.arange(T, dtype=np.float64)
    freqs = t[None, :] * inv_freq[:, None]
    cos, sin = np.cos(freqs), np.sin(freqs)
    base = (half_idx + 0.4 * 64.0) / (1.4 * 64.0)
    power = (t - T // 2) / float(WIN // 2)
    scale = base[:, None] ** power[None, :]
    t1q = cos * scale
    t2q = sign[:, None] * sin * scale
    t1k = cos / scale
    t2k = sign[:, None] * sin / scale
    out = []
    for tb in (t1q, t2q, t1k, t2k):
        tb2 = np.concatenate([tb, tb], 0)
        hi = tb2.astype(np.float32)
        lo = (tb2 - hi.astype(np.float64)).astype(np.float32)
        out.append(np.stack([hi, lo], 0))
    return np.stack(out, 0)


def _consts():
    t_i = np.arange(128)[:, None]
    s_i = np.arange(128)[None, :]
    mask_c = np.where(s_i <= t_i, 0.0, NEG).astype(np.float32)
    mask_a = np.where(t_i <= s_i, 0.0, NEG).astype(np.float32)
    iden = np.eye(128, dtype=np.float32)
    e2 = np.zeros((128, 2), np.float32)
    e2[0:64, 0] = 1.0
    e2[64:128, 1] = 1.0
    e2t = np.zeros((2, 128), np.float32)
    e2t[0, 0:64] = 1.0
    e2t[1, 64:128] = 1.0
    return mask_c, mask_a, iden, e2, e2t


def _stage(inputs):
    f = lambda a: np.ascontiguousarray(np.asarray(a), dtype=np.float32)
    x = f(inputs["L0_attn_x"])
    wdqT = np.ascontiguousarray(f(inputs["L0_attn_W_dq"]).T).reshape(NCH, 128, 128)
    wdkvT = np.ascontiguousarray(f(inputs["L0_attn_W_dkv"]).T).reshape(NCH, 128, 128)
    wuq = f(inputs["L0_attn_W_uq"]).reshape(NCH, 128, 128)
    wuk = f(inputs["L0_attn_W_uk"]).reshape(NCH, 128, 128)
    wuv = f(inputs["L0_attn_W_uv"]).reshape(NCH, 128, 128)
    wdq = f(inputs["L0_attn_W_dq"])
    woT = np.ascontiguousarray(f(inputs["L0_attn_W_o"]).T).reshape(NCH, 128, C)
    wqkv = f(inputs["L0_ffn_wqkv"])
    wqkv3 = np.ascontiguousarray(
        wqkv.reshape(2, NCH, 128, 3, 512).transpose(0, 3, 1, 2, 4))
    wo = f(inputs["L0_ffn_wo"]).reshape(2, 4, 128, C)
    w1 = f(inputs["L0_ffn_w1"])
    w1p = np.zeros((2, C, 2 * HIDP), np.float32)
    w1p[:, :, 0:HID] = w1[:, :, 0:HID]
    w1p[:, :, HIDP:HIDP + HID] = w1[:, :, HID:]
    w1a = np.ascontiguousarray(
        w1p[:, :, 0:HIDP].reshape(2, NCH, 128, NJ, 128).transpose(0, 3, 2, 1, 4))
    w1g = np.ascontiguousarray(
        w1p[:, :, HIDP:].reshape(2, NCH, 128, NJ, 128).transpose(0, 3, 2, 1, 4))
    w2 = f(inputs["L0_ffn_w2"])
    w2p = np.zeros((2, HIDP, C), np.float32)
    w2p[:, 0:HID, :] = w2
    w2b = np.ascontiguousarray(
        w2p.reshape(2, NJ, 128, NCH, 128).transpose(0, 3, 2, 1, 4))
    b1 = f(inputs["L0_ffn_b1"])
    b1p = np.zeros((2, 2 * HIDP), np.float32)
    b1p[:, 0:HID] = b1[:, 0:HID]
    b1p[:, HIDP:HIDP + HID] = b1[:, HID:]
    b1ac = np.ascontiguousarray(b1p[:, 0:HIDP].reshape(2, NJ, 128).transpose(0, 2, 1))
    b1gc = np.ascontiguousarray(b1p[:, HIDP:].reshape(2, NJ, 128).transpose(0, 2, 1))
    rc = lambda a: np.ascontiguousarray(f(a).reshape(2, NCH, 128).transpose(0, 2, 1))
    b2c = rc(inputs["L0_ffn_b2"])
    lgc = rc(inputs["L0_ffn_ln_g"])
    lbc = rc(inputs["L0_ffn_ln_b"])
    qs = f(inputs["L0_ffn_q_scale"])
    ks = f(inputs["L0_ffn_k_scale"])
    p64 = np.arange(128) % 64
    qsc = np.ascontiguousarray(
        np.stack([qs[:, p64], qs[:, p64 ^ 32], ks[:, p64], ks[:, p64 ^ 32]], 2))
    cfull = f(inputs["L0_c"])
    wmodh = f(inputs["L0_wmod"]).reshape(NCH, 128, 6 * C)
    tabs = _rot_tables()
    mask_c, mask_a, iden, e2, e2t = _consts()
    shared = dict(wdqT=wdqT, wuq=wuq, wdkvT=wdkvT, wuk=wuk, wuv=wuv, wdq=wdq,
                  woT=woT, wqkv3=wqkv3, wo4=wo, w1a=w1a, w1g=w1g, w2b=w2b,
                  b1ac=b1ac, b1gc=b1gc, b2c=b2c, lgc=lgc, lbc=lbc, qsc=qsc,
                  wmodh=wmodh, tabs=tabs, maskc=mask_c, maska=mask_a,
                  idenh=iden, e2h=e2, e2th=e2t)
    in_maps = []
    for core in range(NCORES):
        bsl = slice(core * BPC, (core + 1) * BPC)
        m = dict(shared)
        m["xT"] = np.ascontiguousarray(
            x[bsl].transpose(0, 2, 1)).reshape(BPC, NCH, 128, T)
        m["cmodh"] = np.ascontiguousarray(
            cfull[bsl].reshape(BPC, NCH, 128).transpose(2, 1, 0).reshape(128, 16))
        in_maps.append(m)
    return in_maps


def build_program(dbg=()):
    nc = bass.Bass("TRN2", target_bir_lowering=False)
    es = contextlib.ExitStack()
    pr = _Prog(nc, es)
    DI = lambda n, s: nc.dram_tensor(n, s, F32, kind="ExternalInput")
    DO = lambda n, s: nc.dram_tensor(n, s, F32, kind="ExternalOutput")
    DS = lambda n, s: nc.dram_tensor(n, s, F32, kind="Internal")
    SB = lambda n, s, dt=F32: es.enter_context(nc.sbuf_tensor(n, s, dt))
    PS = lambda n, s, dt=F32: es.enter_context(nc.psum_tensor(n, s, dt))

    xT = DI("xT", [BPC, NCH, 128, T])
    d_wdqT = DI("wdqT", [NCH, 128, 128]); d_wuq = DI("wuq", [NCH, 128, 128])
    d_wdkvT = DI("wdkvT", [NCH, 128, 128]); d_wuk = DI("wuk", [NCH, 128, 128])
    d_wuv = DI("wuv", [NCH, 128, 128]); d_wdq = DI("wdq", [128, C])
    d_woT = DI("woT", [NCH, 128, C])
    d_wqkv3 = DI("wqkv3", [2, 3, NCH, 128, 512])
    d_wo4 = DI("wo4", [2, 4, 128, C])
    d_w1a = DI("w1a", [2, NJ, 128, NCH, 128]); d_w1g = DI("w1g", [2, NJ, 128, NCH, 128])
    d_w2b = DI("w2b", [2, NCH, 128, NJ, 128])
    d_b1ac = DI("b1ac", [2, 128, NJ]); d_b1gc = DI("b1gc", [2, 128, NJ])
    d_b2c = DI("b2c", [2, 128, NCH]); d_lgc = DI("lgc", [2, 128, NCH])
    d_lbc = DI("lbc", [2, 128, NCH]); d_qsc = DI("qsc", [2, 128, 4])
    d_cmod = DI("cmodh", [128, 16]); d_wmod = DI("wmodh", [NCH, 128, 6 * C])
    d_tabs = DI("tabs", [4, 2, 128, T])
    d_maskc = DI("maskc", [128, 128]); d_maska = DI("maska", [128, 128])
    d_iden = DI("idenh", [128, 128]); d_e2 = DI("e2h", [128, 2]); d_e2t = DI("e2th", [2, 128])
    outT = DO("outT", [BPC, NCH, 128, T])
    spill = DS("spill", [NCH, 128, T])
    modscr = DS("modscr", [2, 6 * C])
    asbscr = DS("asbscr", [128, NH * 128])
    hsbscr = DS("hsbscr", [128, C])
    dbg_outs = {}

    LG = pr.dctr("LG", 8)
    LS = pr.dctr("LS", 8)
    SO = pr.dctr("SO", 4)
    SG = pr.dctr("SG", 2)
    Tc = pr.ctr("T"); V = pr.ctr("V"); Cc = pr.ctr("C")
    pr.auto["vector"] = V
    pr.auto["scalar"] = Cc

    maskc = SB("s_maskc", [128, 128]); maska = SB("s_maska", [128, 128])
    maskc_b = SB("s_maskcb", [128, 128], BF16); maska_b = SB("s_maskab", [128, 128], BF16)
    idn = SB("s_idn", [128, 128], BF16)
    e2 = SB("s_e2", [128, 2], BF16); e2t = SB("s_e2t", [2, 128], BF16)
    ones1 = SB("s_ones1", [1, 128], BF16)
    onescol = SB("s_onescol", [128, 1], BF16)
    onecol_f = SB("s_onecol_f", [128, 1])
    i1024 = SB("s_i1024", [128, 1])
    i64 = SB("s_i64", [128, 1])
    epscol = SB("s_epscol", [128, 1])
    negone = SB("s_negone", [128, 1])
    modc = SB("s_modc", [128, 96])
    qscol = SB("s_qscol", [128, 8])
    b1acS = SB("s_b1ac", [128, 2 * NJ]); b1gcS = SB("s_b1gc", [128, 2 * NJ])
    b2cS = SB("s_b2c", [128, 2 * NCH]); lgcS = SB("s_lgc", [128, 2 * NCH])
    lbcS = SB("s_lbc", [128, 2 * NCH])
    statc = SB("s_statc", [128, 32])
    diag2 = SB("s_diag2", [128, 512], BF16)
    xb = SB("s_xb", [128, NCH * T])

    def mcol(v, b, ch):
        return modc[:, v * 16 + b * 8 + ch: v * 16 + b * 8 + ch + 1]

    pr.dma("sync", maskc[:, :], d_maskc[:, :], LS)
    pr.dma("sync", maska[:, :], d_maska[:, :], LS)
    pr.dma("gpsimd", idn[:, :], d_iden[:, :], LG)
    pr.dma("gpsimd", maskc_b[:, :], d_maskc[:, :], LG)
    pr.dma("gpsimd", maska_b[:, :], d_maska[:, :], LG)
    pr.dma("gpsimd", e2[:, :], d_e2[:, :], LG)
    pr.dma("gpsimd", e2t[:, :], d_e2t[:, :], LG)
    for L in range(2):
        pr.dma("sync", qscol[:, 4 * L:4 * L + 4], d_qsc[L], LS)
        pr.dma("sync", b1acS[:, NJ * L:NJ * (L + 1)], d_b1ac[L], LS)
        pr.dma("sync", b1gcS[:, NJ * L:NJ * (L + 1)], d_b1gc[L], LS)
        pr.dma("sync", b2cS[:, NCH * L:NCH * (L + 1)], d_b2c[L], LS)
        pr.dma("sync", lgcS[:, NCH * L:NCH * (L + 1)], d_lgc[L], LS)
        pr.dma("sync", lbcS[:, NCH * L:NCH * (L + 1)], d_lbc[L], LS)
    pr.op("vector", "memset", ones1[:, :], 1.0, inc=[V])
    pr.op("vector", "memset", onescol[:, :], 1.0, inc=[V])
    pr.op("vector", "memset", onecol_f[:, :], 1.0, inc=[V])
    pr.op("vector", "memset", i1024[:, :], 1.0 / 1024.0, inc=[V])
    pr.op("vector", "memset", i64[:, :], 1.0 / 64.0, inc=[V])
    pr.op("vector", "memset", epscol[:, :], 1.0e-5, inc=[V])
    pr.op("vector", "memset", negone[:, :], -1.0, inc=[V])
    consts_v = V.at()
    consts_l = (LS.at(), LG.at())

    if _RUN["mod"]:
      with contextlib.ExitStack() as st0:
        SBs = lambda n, s, dt=F32: st0.enter_context(nc.sbuf_tensor(n, s, dt))
        PSs = lambda n, s: st0.enter_context(nc.psum_tensor(n, s, F32))
        cmods = SBs("u_cmods", [128, 16])
        cmodf = SBs("u_cmodf", [128, 16])
        wms = [SBs(f"u_wms{i}", [128, 6 * C]) for i in range(2)]
        modrow = SBs("u_modrow", [2, 6 * C])
        mscr = SBs("u_mscr", [2, C])
        psmod = [PSs(f"u_psmod{i}", [2, 512]) for i in range(2)]
        pr.dma("sync", cmodf[:, :], d_cmod[:, :], LS)
        pr.op("scalar", "activation", cmods[:, :], cmodf[:, :], AF.Silu,
              wait=[LS.at()], inc=[Cc])
        pr.op("vector", "memset", modrow[:, :], 0.0, inc=[V])
        silu_c = Cc.at()
        mrow_v = V.at()
        mm_done, ev_done, ld_done = {}, {}, {}
        for k in range(NCH):
            par = k % 2
            w = [consts_v] + ([(Tc.sem, mm_done[par])] if par in mm_done else [])
            ld_done[par] = pr.dma("sync", wms[par][:, :], d_wmod[k], LS, wait=w)
            for n in range(12):
                ppar = n % 2
                w = [silu_c] + ld_done[par]
                if ppar in ev_done:
                    w.append((V.sem, ev_done[ppar]))
                pr.op("tensor", "matmul", psmod[ppar][:, :],
                      cmods[:, 2 * k:2 * k + 2], wms[par][:, 512 * n:512 * (n + 1)],
                      start=True, stop=True, wait=w, inc=[Tc])
                pr.op("vector", "tensor_tensor", modrow[:, 512 * n:512 * (n + 1)],
                      modrow[:, 512 * n:512 * (n + 1)], psmod[ppar][:, :], ALU.add,
                      wait=[Tc.at(), mrow_v], inc=[V])
                ev_done[ppar] = V.n
            mm_done[par] = Tc.n
        for lo in (2 * C, 5 * C):
            seg = modrow[:, lo:lo + C]
            pr.op("vector", "tensor_single_scalar", mscr[:, :], seg, 0.0, ALU.is_ge,
                  inc=[V])
            pr.bar("vector")
            pr.op("vector", "tensor_scalar", mscr[:, :], mscr[:, :], 2.0, -1.0,
                  ALU.mult, ALU.add, inc=[V])
            pr.op("scalar", "activation", seg, seg, AF.Abs, wait=[V.at()], inc=[Cc])
            pr.op("vector", "tensor_scalar", seg, seg, 0.25, None, ALU.max,
                  wait=[Cc.at()], inc=[V])
            pr.bar("vector")
            pr.op("vector", "tensor_tensor", seg, seg, mscr[:, :], ALU.mult, inc=[V])
        pr.dma("sync", modscr[:, :], modrow[:, :], SO, wait=[V.at()])
        mod_st = SO.at()
        for v in range(6 if _RUN["modcols"] else 0):
            for b in range(BPC):
                for ch in range(NCH):
                    pr.dma("sync", mcol(v, b, ch),
                           modscr[b:b + 1, v * C + ch * 128: v * C + (ch + 1) * 128]
                           .rearrange("o p -> p o"), LS,
                           wait=[mod_st], allow_slow_non_contiguous=True)
        pr.op("vector", "tensor_scalar", modc[:, 16:32], modc[:, 16:32], 1.0, None,
              ALU.add, wait=[LS.at()], inc=[V])
        pr.op("vector", "tensor_scalar", modc[:, 64:80], modc[:, 64:80], 1.0, None,
              ALU.add, inc=[V])
    mod_ready = V.at()

    asb_st = None
    if _RUN["aw"]:
      with contextlib.ExitStack() as st1:
        SBs = lambda n, s, dt=BF16: st1.enter_context(nc.sbuf_tensor(n, s, dt))
        PSs = lambda n, s: st1.enter_context(nc.psum_tensor(n, s, F32))
        wuq_b = SBs("u_wuq", [128, C]); wuk_b = SBs("u_wuk", [128, C])
        wuv_b = SBs("u_wuv", [128, C]); wdq_b = SBs("u_wdq", [128, C])
        wotb = [SBs(f"u_wot{i}", [128, C]) for i in range(2)]
        m1b = SBs("u_m1b", [128, 128]); gsb = SBs("u_gsb", [128, C])
        asb = SBs("u_asb", [128, NH * 128]); hsb = SBs("u_hsb", [128, C])
        psa = PSs("u_psa", [128, 128])
        psh = PSs("u_psh", [128, C])
        aw_pg = [Tc.at(), V.at(), Cc.at(), SO.at(), LS.at()]
        for ch in range(NCH):
            pr.dma("gpsimd", wuq_b[:, 128 * ch:128 * (ch + 1)], d_wuq[ch], LG,
                   wait=aw_pg)
            pr.dma("gpsimd", wuk_b[:, 128 * ch:128 * (ch + 1)], d_wuk[ch], LG)
            pr.dma("gpsimd", wuv_b[:, 128 * ch:128 * (ch + 1)], d_wuv[ch], LG)
        pr.dma("gpsimd", wdq_b[:, :], d_wdq[:, :], LG)
        aw_l = LG.at()
        for ch in range(NCH):
            pr.op("tensor", "matmul", psa[:, :], wuq_b[:, 128 * ch:128 * (ch + 1)],
                  wuk_b[:, 128 * ch:128 * (ch + 1)], start=(ch == 0),
                  stop=(ch == NCH - 1), wait=[aw_l], inc=[Tc])
        pr.op("vector", "tensor_copy", m1b[:, :], psa[:, :], wait=[Tc.at()], inc=[V])
        for ch in range(NCH):
            pr.op("tensor", "matmul", psa[:, :], wdq_b[:, 128 * ch:128 * (ch + 1)],
                  m1b[:, :], start=True, stop=True, wait=[V.at()], inc=[Tc])
            pr.op("vector", "tensor_copy", gsb[:, 128 * ch:128 * (ch + 1)], psa[:, :],
                  wait=[Tc.at()], inc=[V])
        for h in range(NH):
            hl = 64 * (h % 2)
            ch = h // 2
            pr.op("tensor", "matmul", psa[:, :],
                  wuq_b[hl:hl + 64, 128 * ch:128 * (ch + 1)],
                  gsb[hl:hl + 64, 128 * ch:128 * (ch + 1)],
                  start=True, stop=True, tile_position=(hl, 0),
                  wait=[V.at()], inc=[Tc])
            pr.op("vector", "tensor_scalar", asb[:, 128 * h:128 * (h + 1)], psa[:, :],
                  0.125, None, ALU.mult, wait=[Tc.at()], inc=[V])
        for ch in range(NCH):
            par = ch % 2
            w = [(Tc, Tc.n)] if ch >= 2 else []
            pr.dma("gpsimd", wotb[par][:, :], d_woT[ch], LG, wait=w)
            for n2 in range(2):
                pr.op("tensor", "matmul", psh[:, 512 * n2:512 * (n2 + 1)],
                      wuv_b[:, 128 * ch:128 * (ch + 1)],
                      wotb[par][:, 512 * n2:512 * (n2 + 1)],
                      start=(ch == 0), stop=(ch == NCH - 1),
                      wait=[LG.at()], inc=[Tc])
        pr.op("vector", "tensor_copy", hsb[:, :], psh[:, :], wait=[Tc.at()], inc=[V])
        pr.dma("gpsimd", asbscr[:, :], asb[:, :], SG, wait=[V.at()])
        pr.dma("gpsimd", hsbscr[:, :], hsb[:, :], SG)
        asb_st = SG.at()
        if "asb" in dbg:
            dbg_outs["asb"] = DO("dbg_asb", [128, NH * 128])
            pr.dma("gpsimd", dbg_outs["asb"][:, :], asb[:, :], SG, wait=[V.at()])
            dbg_outs["hsb"] = DO("dbg_hsb", [128, C])
            pr.dma("gpsimd", dbg_outs["hsb"][:, :], hsb[:, :], SG, wait=[V.at()])
    if "modc" in dbg:
        dbg_outs["modc"] = DO("dbg_modc", [128, 96])
        pr.dma("sync", dbg_outs["modc"][:, :], modc[:, :], SO, wait=[mod_ready])

    ps_sc = PS("ps_sc", [128, T])
    ps_a = PS("ps_a", [128, 512])
    ps_b = PS("ps_b", [128, 512])
    ps_pt = PS("ps_pt", [128, 512])
    ps_cx = PS("ps_cx", [128, 512])
    ps_r1 = ps_pt[0:1, 0:512]
    ps_s2 = ps_pt[0:2, 0:512]

    guard = {}

    def gw(key):
        return [guard[key]] if key in guard else []

    def pg():
        return [Tc.at(), V.at(), Cc.at(), LS.at(), LG.at(), SO.at(), SG.at()]

    def chunk(buf, ch, t0=0, tl=T):
        return buf[:, ch * T + t0: ch * T + t0 + tl]

    def ln_block(st, src, shifted, cast_to, cast_stride, out_to, out_stride,
                 gcol, bcol, t0, tl, wait0=(), norm_wait=(), edge=None):
        nt = tl // 512
        RSTD, MURS = st["RSTD"], st["MURS"]
        srow, qrow, sqb = st["srow"], st["qrow"], st["sqb"]
        muf, sqf, m2f, tmpf = st["muf"], st["sqf"], st["m2f"], st["tmpf"]
        for ch in range(NCH):
            dst = cast_to[:, ch * cast_stride: ch * cast_stride + tl]
            if shifted and ch >= 4:
                if t0 == 0:
                    pr.op("vector", "memset", dst[:, 0:1], 0.0, wait=list(wait0), inc=[V])
                    pr.op("vector", "tensor_copy", dst[:, 1:tl],
                          src[:, ch * T: ch * T + tl - 1], inc=[V])
                elif edge is not None:
                    pr.op("vector", "tensor_copy", dst[:, 0:1], edge(ch),
                          wait=list(wait0), inc=[V])
                    pr.op("vector", "tensor_copy", dst[:, 1:tl],
                          src[:, ch * T + t0: ch * T + t0 + tl - 1], inc=[V])
                else:
                    pr.op("vector", "tensor_copy", dst,
                          src[:, ch * T + t0 - 1: ch * T + t0 + tl - 1],
                          wait=list(wait0), inc=[V])
            else:
                pr.op("vector", "tensor_copy", dst,
                      src[:, ch * T + t0: ch * T + t0 + tl], wait=list(wait0), inc=[V])
        cast_v = V.at()
        for n in range(nt):
            for ch in range(NCH):
                rhs = cast_to[:, ch * cast_stride + n * 512: ch * cast_stride + (n + 1) * 512]
                pr.op("tensor", "matmul", ps_r1, onescol[:, :], rhs,
                      start=(ch == 0), stop=(ch == NCH - 1),
                      wait=[cast_v, consts_v] + gw("ps_pt"), inc=[Tc])
            pr.op("vector", "tensor_copy", srow[:, n * 512:(n + 1) * 512], ps_r1,
                  wait=[Tc.at()], inc=[V])
            guard["ps_pt"] = (V, V.n)
            for ch in range(NCH):
                rhs = cast_to[:, ch * cast_stride + n * 512: ch * cast_stride + (n + 1) * 512]
                pr.op("scalar", "activation", sqb[:, :], rhs, AF.Square,
                      wait=[cast_v] + gw("sqb"), inc=[Cc])
                pr.op("tensor", "matmul", ps_r1, onescol[:, :], sqb[:, :],
                      start=(ch == 0), stop=(ch == NCH - 1),
                      wait=[Cc.at()] + gw("ps_pt"), inc=[Tc])
                guard["sqb"] = (Tc, Tc.n)
            pr.op("vector", "tensor_copy", qrow[:, n * 512:(n + 1) * 512], ps_r1,
                  wait=[Tc.at()], inc=[V])
            guard["ps_pt"] = (V, V.n)
            pr.op("tensor", "matmul", ps_a[:, :], ones1[:, :],
                  srow[:, n * 512:(n + 1) * 512], start=True, stop=True,
                  wait=[V.at()] + gw("ps_a"), inc=[Tc])
            pr.op("tensor", "matmul", ps_b[:, :], ones1[:, :],
                  qrow[:, n * 512:(n + 1) * 512], start=True, stop=True,
                  wait=[V.at()] + gw("ps_b"), inc=[Tc])
            bc_v = Tc.at()
            pr.op("vector", "tensor_scalar_mul", muf[:, :], ps_a[:, :], i1024[:, :],
                  wait=[bc_v], inc=[V])
            pr.op("vector", "tensor_scalar_mul", sqf[:, :], ps_b[:, :], i1024[:, :],
                  inc=[V])
            guard["ps_a"] = (V, V.n)
            guard["ps_b"] = (V, V.n)
            pr.op("scalar", "activation", m2f[:, :], muf[:, :], AF.Square,
                  wait=[V.at()], inc=[Cc])
            pr.op("vector", "tensor_tensor", sqf[:, :], sqf[:, :], m2f[:, :],
                  ALU.subtract, wait=[Cc.at()], inc=[V])
            pr.op("scalar", "activation", m2f[:, :], sqf[:, :], AF.Sqrt,
                  bias=epscol[:, :], scale=1.0, wait=[V.at()], inc=[Cc])
            pr.op("vector", "reciprocal", RSTD[:, n * 512:(n + 1) * 512],
                  m2f[:, :], wait=[Cc.at()], inc=[V], lowp=True)
            pr.op("vector", "tensor_tensor", MURS[:, n * 512:(n + 1) * 512],
                  muf[:, :], RSTD[:, n * 512:(n + 1) * 512], ALU.mult, inc=[V])
        stats_v = V.at()
        for ch in range(NCH):
            msrc = cast_to[:, ch * cast_stride: ch * cast_stride + tl]
            pr.op("vector", "tensor_tensor", tmpf[:, 0:tl], msrc,
                  RSTD[:, 0:tl], ALU.mult, wait=[stats_v] + list(norm_wait), inc=[V])
            dst = out_to[:, ch * out_stride: ch * out_stride + tl]
            pr.op("vector", "tensor_tensor", dst, tmpf[:, 0:tl],
                  MURS[:, 0:tl], ALU.subtract, inc=[V])
            pr.op("vector", "tensor_scalar", dst, dst, gcol(ch), bcol(ch),
                  ALU.mult, ALU.add, inc=[V])
        return V.at()

    def mk_stats(scope, tag):
        A = lambda n, sh, dt=F32: scope.enter_context(nc.sbuf_tensor(n + tag, sh, dt))
        return dict(RSTD=A("RSTD", [128, T], BF16), MURS=A("MURS", [128, T], BF16),
                    srow=A("srow", [1, T], BF16), qrow=A("qrow", [1, T], BF16),
                    sqb=A("sqb", [128, 512], BF16), muf=A("muf", [128, 512]),
                    sqf=A("sqf", [128, 512]), m2f=A("m2f", [128, 512]),
                    tmpf=A("tmpf", [128, T]))

    x1_v = None
    spill_s = None
    for b in range(_RUN["batches"]):
        with contextlib.ExitStack() as scA:
            A = lambda n, sh, dt=BF16: scA.enter_context(
                nc.sbuf_tensor(f"{n}_b{b}", sh, dt))
            stA = mk_stats(scA, f"_A{b}")
            nA = A("nA", [128, NCH * T])
            wdqT_b = A("wdqT_b", [128, C]); wdkvT_b = A("wdkvT_b", [128, C])
            asbL = A("asbL", [128, NH * 128]); hsbL = A("hsbL", [128, C])
            ckvT = A("ckvT", [128, T]); ckvTM = A("ckvTM", [128, T])
            qlatT = A("qlatT", [128, T]); qlT = A("qlT", [128, T])
            paO = A("paO", [128, 3 * T]); ptG = A("ptG", [128, 16 * 512])
            ctxT = A("ctxT", [128, 2 * T])
            PG = pg()
            for ch in range(NCH if _RUN["loads"] else 0):
                pr.dma("sync", chunk(xb, ch), xT[b, ch], LS, wait=PG + gw("xb"))
            for ch in range(NCH if _RUN["loads"] else 0):
                pr.dma("gpsimd", wdqT_b[:, 128 * ch:128 * (ch + 1)], d_wdqT[ch],
                       LG, wait=PG)
                pr.dma("gpsimd", wdkvT_b[:, 128 * ch:128 * (ch + 1)], d_wdkvT[ch], LG)
            if _RUN["loads"] and _RUN["aw"]:
                pr.dma("gpsimd", asbL[:, :], asbscr[:, :], LG, wait=PG + [asb_st])
                pr.dma("gpsimd", hsbL[:, :], hsbscr[:, :], LG)
            x_l = LS.at()
            w_l = LG.at()

            if not _RUN["p1"]:
                continue
            n1_v = ln_block(stA, xb, False, nA, T, nA, T,
                            lambda ch: mcol(1, b, ch), lambda ch: mcol(0, b, ch),
                            0, T, wait0=[x_l, mod_ready] + PG)

            if not _RUN["p3"]:
                continue
            for n in range(4):
                for ch in range(NCH):
                    pr.op("tensor", "matmul", ps_a[:, :],
                          wdkvT_b[:, 128 * ch:128 * (ch + 1)],
                          chunk(nA, ch, n * 512, 512), start=(ch == 0),
                          stop=(ch == NCH - 1), wait=[n1_v, w_l] + gw("ps_a"), inc=[Tc])
                pr.op("vector", "tensor_copy", ckvT[:, n * 512:(n + 1) * 512],
                      ps_a[:, :], wait=[Tc.at()], inc=[V])
                guard["ps_a"] = (V, V.n)
                for ch in range(NCH):
                    pr.op("tensor", "matmul", ps_b[:, :],
                          wdqT_b[:, 128 * ch:128 * (ch + 1)],
                          chunk(nA, ch, n * 512, 512), start=(ch == 0),
                          stop=(ch == NCH - 1), wait=[n1_v, w_l] + gw("ps_b"), inc=[Tc])
                pr.op("vector", "tensor_copy", qlatT[:, n * 512:(n + 1) * 512],
                      ps_b[:, :], wait=[Tc.at()], inc=[V])
                guard["ps_b"] = (V, V.n)
            ckv_v = V.at()
            for j in range(16):
                pr.op("tensor", "matmul", ps_pt[:, 0:128],
                      ckvT[:, 128 * j:128 * (j + 1)], idn[:, :], start=True, stop=True,
                      wait=[ckv_v, consts_l[1]] + gw("ps_pt"), inc=[Tc])
                pr.op("vector", "tensor_copy", ckvTM[:, 128 * j:128 * (j + 1)],
                      ps_pt[:, 0:128], wait=[Tc.at()], inc=[V])
                guard["ps_pt"] = (V, V.n)
            ckvtm_v = V.at()
            if "ckvT" in dbg and b == 0:
                dbg_outs["ckvT"] = DO("dbg_ckvT", [128, T])
                pr.dma("gpsimd", dbg_outs["ckvT"][:, :], ckvT[:, :], SG, wait=[ckv_v])
                dbg_outs["qlatT"] = DO("dbg_qlatT", [128, T])
                pr.dma("gpsimd", dbg_outs["qlatT"][:, :], qlatT[:, :], SG, wait=[ckv_v])
                dbg_outs["n1"] = DO("dbg_n1", [128, NCH * T])
                guard["nAdbg"] = pr.dma("gpsimd", dbg_outs["n1"][:, :], nA[:, :], SG,
                                        wait=[n1_v])

            for h in range(NH if _RUN["p4"] else 0):
                for n in range(4):
                    pr.op("tensor", "matmul", ps_b[:, :], asbL[:, 128 * h:128 * (h + 1)],
                          qlatT[:, 512 * n:512 * (n + 1)], start=True, stop=True,
                          wait=[ckv_v, w_l] + gw("ps_b") + gw(("pt", 1)), inc=[Tc])
                    pr.op("vector", "tensor_copy", qlT[:, 512 * n:512 * (n + 1)],
                          ps_b[:, :], wait=[Tc.at()], inc=[V])
                    guard["ps_b"] = (V, V.n)
                ql_v = V.at()
                pending = []
                pt_state = {}
                ptGv = ptG[:, :].rearrange("p (j s) -> p j s", s=512)

                def flush_pt(prev):
                    ti, par, rc_c = prev
                    slen = 128 * (ti + 1)
                    pr.op("vector", "tensor_scalar",
                          paO[:, 2048 * par:2048 * par + slen],
                          paO[:, 2048 * par:2048 * par + slen],
                          statc[:, 8 * par + 6:8 * par + 7], None, ALU.mult,
                          wait=[(Cc, rc_c)], inc=[V])
                    norm_v = V.n
                    sub = ti % 4
                    for gi, j0 in enumerate(range(0, ti + 1, 4)):
                        jn = min(4, ti + 1 - j0)
                        gpar = gi % 2
                        bank = ps_pt if gpar == 0 else ps_b
                        for j in range(j0, j0 + jn):
                            pr.op("tensor", "matmul",
                                  bank[:, 128 * (j - j0):128 * (j - j0 + 1)],
                                  paO[:, 2048 * par + 128 * j:2048 * par + 128 * (j + 1)],
                                  idn[:, :], start=True, stop=True,
                                  wait=[(V.sem, norm_v)] + gw(("pt", gpar)) +
                                       (gw("ps_b") if gpar == 1 else []), inc=[Tc])
                        if gi % 2 == 0:
                            pr.op("vector", "tensor_copy",
                                  ptGv[:, j0:j0 + jn, 128 * sub:128 * (sub + 1)],
                                  bank[:, 0:128 * jn].rearrange("p (j s) -> p j s", s=128),
                                  wait=[Tc.at()], inc=[V])
                            guard[("pt", gpar)] = (V, V.n)
                        else:
                            pr.op("scalar", "copy",
                                  ptGv[:, j0:j0 + jn, 128 * sub:128 * (sub + 1)],
                                  bank[:, 0:128 * jn].rearrange("p (j s) -> p j s", s=128),
                                  wait=[Tc.at()], inc=[Cc])
                            guard[("pt", gpar)] = (Cc, Cc.n)
                    pt_state[par] = (Tc, Tc.n)

                def flush_pv(g):
                    pt_c = V.at()
                    pt_c2 = Cc.at()
                    tmax = 4 * g + 3
                    for j in range(tmax + 1):
                        lo = max(0, j - 4 * g)
                        nvalid = 4 - lo
                        pr.op("tensor", "matmul",
                              ps_cx[:, 128 * lo:128 * (lo + nvalid)],
                              ckvTM[:, 128 * j:128 * (j + 1)],
                              ptG[:, 512 * j + 128 * lo:512 * j + 128 * (lo + nvalid)],
                              start=(j == 0), stop=(j == tmax),
                              wait=[pt_c, pt_c2, ckvtm_v] + gw("ps_cx"), inc=[Tc])
                    pr.op("scalar", "copy",
                          ctxT[:, (h % 2) * T + 512 * g:(h % 2) * T + 512 * (g + 1)],
                          ps_cx[:, :], wait=[Tc.at()], inc=[Cc])
                    guard["ps_cx"] = (Cc, Cc.n)

                for ti in range(16):
                    par = ti % 3
                    slen = 128 * (ti + 1)
                    nmm = (slen + 511) // 512
                    for n in range(nmm):
                        wd = min(512, slen - 512 * n)
                        pr.op("tensor", "matmul", ps_sc[:, 512 * n:512 * n + wd],
                              qlT[:, 128 * ti:128 * (ti + 1)],
                              ckvT[:, 512 * n:512 * n + wd], start=True,
                              stop=(n != nmm - 1),
                              wait=[ql_v] + gw("ps_sc") + gw(("sc", 0)) + gw(("sc", 1)),
                              inc=[Tc])
                        if n == nmm - 1:
                            pr.op("tensor", "matmul", ps_sc[:, slen - 128:slen],
                                  idn[:, :], maskc_b[:, :], start=False, stop=True,
                                  wait=[consts_l[1]], inc=[Tc])
                        pr.op("vector", "tensor_reduce",
                              statc[:, 8 * par + n:8 * par + n + 1],
                              ps_sc[:, 512 * n:512 * n + wd], axis=AXX, op=ALU.max,
                              wait=[Tc.at()] + gw(("st", par)), inc=[V])
                    pr.bar("vector")
                    pr.op("vector", "tensor_reduce", statc[:, 8 * par + 4:8 * par + 5],
                          statc[:, 8 * par:8 * par + nmm], axis=AXX, op=ALU.max,
                          negate=True, inc=[V])
                    mask_v = V.at()
                    pr.op("scalar", "activation",
                          paO[:, 2048 * par:2048 * par + slen], ps_sc[:, 0:slen],
                          AF.Exp, bias=statc[:, 8 * par + 4:8 * par + 5],
                          scale=onecol_f[:, :],
                          accum_out=statc[:, 8 * par + 5:8 * par + 6],
                          wait=[mask_v] + ([pt_state[par]] if par in pt_state else []),
                          inc=[Cc])
                    guard["ps_sc"] = (Cc, Cc.n)
                    guard[("st", par)] = (Cc, Cc.n)
                    pr.op("scalar", "activation", statc[:, 8 * par + 7:8 * par + 8],
                          statc[:, 8 * par + 5:8 * par + 6], AF.Ln, inc=[Cc])
                    pr.op("scalar", "activation", statc[:, 8 * par + 6:8 * par + 7],
                          statc[:, 8 * par + 7:8 * par + 8], AF.Exp,
                          scale=negone[:, :], inc=[Cc])
                    pending.append((ti, par, Cc.n))
                    if len(pending) > 2:
                        p_ = pending.pop(0)
                        flush_pt(p_)
                        if p_[0] % 4 == 3:
                            flush_pv(p_[0] // 4)
                for p_ in pending:
                    flush_pt(p_)
                    if p_[0] % 4 == 3:
                        flush_pv(p_[0] // 4)
                if h % 2 == 1:
                    jj = h // 2
                    ctx_v = Cc.at()
                    for n in range(4):
                        pr.op("tensor", "matmul", ps_a[0:64, :],
                              hsbL[:, 128 * jj:128 * jj + 64],
                              ctxT[:, 512 * n:512 * (n + 1)], start=True, stop=True,
                              tile_position=(0, 0),
                              wait=[ctx_v, w_l] + gw("ps_a"), inc=[Tc])
                        pr.op("tensor", "matmul", ps_a[64:128, :],
                              hsbL[:, 128 * jj + 64:128 * (jj + 1)],
                              ctxT[:, T + 512 * n:T + 512 * (n + 1)], start=True,
                              stop=True, tile_position=(0, 64), inc=[Tc])
                        pr.op("vector", "scalar_tensor_tensor",
                              chunk(xb, jj, 512 * n, 512), ps_a[:, :], mcol(2, b, jj),
                              chunk(xb, jj, 512 * n, 512), ALU.mult, ALU.add,
                              wait=[Tc.at()], inc=[V])
                        guard["ps_a"] = (V, V.n)
            attn_v = V.at()
            if "x_attn" in dbg and b == 0:
                dbg_outs["x_attn"] = DO("dbg_x_attn", [128, NCH * T])
                guard["xbdbg"] = pr.dma("sync", dbg_outs["x_attn"][:, :], xb[:, :], SO,
                                        wait=[attn_v])

            for ch in range(NCH):
                pr.dma("sync", spill[ch], chunk(xb, ch),
                       SO, wait=[attn_v] + gw("spill"))
            spill_s = SO.at()
            x1_v = ln_block(stA, xb, False, nA, T, xb, T,
                            lambda ch: mcol(4, b, ch), lambda ch: mcol(3, b, ch),
                            0, T, wait0=[(Tc, Tc.n)] + gw("nAdbg"),
                            norm_wait=[spill_s] + gw("xbdbg"))
            if "x1h" in dbg and b == 0:
                dbg_outs["x1h"] = DO("dbg_x1h", [128, NCH * T])
                pr.dma("sync", dbg_outs["x1h"][:, :], xb[:, :], SO, wait=[x1_v])

        for L in range(_RUN["layers"]):
            with contextlib.ExitStack() as scB:
                A = lambda n, sh, dt=BF16: scB.enter_context(
                    nc.sbuf_tensor(f"{n}_b{b}L{L}", sh, dt))
                tfold = A("tfold", [128, 4 * T])
                vtm = A("vtm", [128, 64 * 128])
                nAm = A("nAm", [128, NCH * T])
                nB = A("nB", [128, 4 * T])
                nC = A("nC", [128, 4 * T])
                PG = pg()
                with contextlib.ExitStack() as scT:
                    thl = scT.enter_context(
                        nc.sbuf_tensor(f"thl_b{b}L{L}", [128, 2 * T], F32))
                    for tb in range(4):
                        pr.dma("sync", thl[:, 0:T], d_tabs[tb, 0],
                               LS, wait=PG + gw("thl"))
                        pr.dma("sync", thl[:, T:2 * T], d_tabs[tb, 1], LS,
                               wait=PG + gw("thl"))
                        pr.op("vector", "tensor_tensor", thl[:, 0:T], thl[:, 0:T],
                              thl[:, T:2 * T], ALU.add, wait=[LS.at()] + PG, inc=[V])
                        pr.op("vector", "tensor_scalar_mul",
                              tfold[:, tb * T:(tb + 1) * T], thl[:, 0:T],
                              qscol[:, 4 * L + tb:4 * L + tb + 1],
                              wait=[consts_l[0]], inc=[V])
                        guard["thl"] = (V, V.n)
                fold_v = V.at()
                for ch in range(NCH):
                    dst = chunk(nAm, ch)
                    if ch >= 4:
                        pr.op("vector", "memset", dst[:, 0:1], 0.0,
                              wait=[x1_v] + PG, inc=[V])
                        pr.op("vector", "tensor_copy", dst[:, 1:T],
                              xb[:, ch * T: ch * T + T - 1], inc=[V])
                    else:
                        pr.op("vector", "tensor_copy", dst, chunk(xb, ch),
                              wait=[x1_v] + PG, inc=[V])
                mb_v = V.at()
                with contextlib.ExitStack() as scQ:
                    AQ = lambda n, sh, dt=BF16: scQ.enter_context(
                        nc.sbuf_tensor(f"{n}_b{b}L{L}", sh, dt))
                    segw = AQ("segw", [128, NCH * 512])
                    PGQ = pg()
                    sqbA = AQ("sqbA", [128, T])
                    ssrow = AQ("ssrow", [2, T])
                    rotb = AQ("rotb", [128, T])
                    tmpa = AQ("tmpa", [128, T])
                    rbuf = AQ("rbuf", [128, T])
                    m2fA = AQ("m2fA", [128, 512], F32)
                    for pss in range(3):
                        for k in range(NCH):
                            pr.dma("gpsimd", segw[:, 512 * k:512 * (k + 1)],
                                   d_wqkv3[L, pss, k],
                                   LG, wait=PGQ + gw(("segw", k)))
                        seg_l = LG.at()
                        for oc in range(4):
                            for n in range(4):
                                for k in range(NCH):
                                    pr.op("tensor", "matmul", ps_a[:, :],
                                          segw[:, 512 * k + 128 * oc:512 * k + 128 * (oc + 1)],
                                          chunk(nAm, k, 512 * n, 512),
                                          start=(k == 0), stop=(k == NCH - 1),
                                          wait=[seg_l, mb_v] + gw("ps_a"), inc=[Tc])
                                mm_t = Tc.at()
                                if pss == 0:
                                    pr.op("vector", "tensor_copy",
                                          nB[:, oc * T + 512 * n:oc * T + 512 * (n + 1)],
                                          ps_a[:, :], wait=[mm_t], inc=[V])
                                    guard["ps_a"] = (V, V.n)
                                elif pss == 1:
                                    pr.op("vector", "tensor_copy",
                                          nC[:, oc * T + 512 * n:oc * T + 512 * (n + 1)],
                                          ps_a[:, :], wait=[mm_t], inc=[V])
                                    guard["ps_a"] = (V, V.n)
                                else:
                                    pr.op("vector", "tensor_copy", tmpa[:, 0:512],
                                          ps_a[:, :], wait=[mm_t] + gw("tmpa"), inc=[V])
                                    guard["ps_a"] = (V, V.n)
                                    for j in range(4):
                                        tj = 4 * n + j
                                        pr.op("tensor", "matmul", ps_pt[:, 0:128],
                                              tmpa[:, 128 * j:128 * (j + 1)], idn[:, :],
                                              start=True, stop=True,
                                              wait=[V.at()] + gw("ps_pt"), inc=[Tc])
                                        pr.op("vector", "tensor_copy",
                                              vtm[:, (4 * tj + oc) * 128:(4 * tj + oc + 1) * 128],
                                              ps_pt[:, 0:128], wait=[Tc.at()], inc=[V])
                                        guard["ps_pt"] = (V, V.n)
                                    guard["tmpa"] = (Tc, Tc.n)
                        for k in range(NCH):
                            guard[("segw", k)] = (Tc, Tc.n)
                    qkv_v = V.at()
                    for side, buf in ((0, nB), (1, nC)):
                        for cc in range(4):
                            src = buf[:, cc * T:(cc + 1) * T]
                            pr.op("scalar", "activation", sqbA[:, :], src, AF.Square,
                                  wait=[qkv_v] + gw("sqbA"), inc=[Cc])
                            for n in range(4):
                                pr.op("tensor", "matmul", ps_s2, e2[:, :],
                                      sqbA[:, 512 * n:512 * (n + 1)], start=True,
                                      stop=True,
                                      wait=[Cc.at(), consts_l[1]] + gw("ps_pt"),
                                      inc=[Tc])
                                pr.op("vector", "tensor_copy",
                                      ssrow[:, 512 * n:512 * (n + 1)], ps_s2,
                                      wait=[Tc.at()], inc=[V])
                                guard["ps_pt"] = (V, V.n)
                            guard["sqbA"] = (Tc, Tc.n)
                            for n in range(4):
                                pr.op("tensor", "matmul", ps_b[:, :], e2t[:, :],
                                      ssrow[:, 512 * n:512 * (n + 1)], start=True,
                                      stop=True,
                                      wait=[V.at()] + gw("ps_b") + gw(("pt", 1)),
                                      inc=[Tc])
                                sc = i64[:, :] if side == 0 else onecol_f[:, :]
                                pr.op("scalar", "activation", m2fA[:, :], ps_b[:, :],
                                      AF.Sqrt, scale=sc, wait=[Tc.at(), V.at()],
                                      inc=[Cc])
                                guard["ps_b"] = (Cc, Cc.n)
                                clipv = 1.25e-13 if side == 0 else 1.0e-12
                                pr.op("vector", "tensor_scalar", m2fA[:, :], m2fA[:, :],
                                      clipv, None, ALU.max, wait=[Cc.at()], inc=[V])
                                pr.op("vector", "reciprocal",
                                      rbuf[:, 512 * n:512 * (n + 1)], m2fA[:, :],
                                      inc=[V], lowp=True)
                            for g0, g1 in ((0, 32), (32, 0), (64, 96), (96, 64)):
                                pr.op("vector", "tensor_copy", rotb[g0:g0 + 32, :],
                                      buf[g1:g1 + 32, cc * T:(cc + 1) * T], inc=[V])
                            tb0 = (2 * side) * T
                            tb1 = (2 * side + 1) * T
                            pr.op("vector", "tensor_tensor", tmpa[:, :], src,
                                  tfold[:, tb0:tb0 + T], ALU.mult,
                                  wait=[fold_v, (Tc, Tc.n)], inc=[V])
                            pr.op("vector", "tensor_tensor", rotb[:, :], rotb[:, :],
                                  tfold[:, tb1:tb1 + T], ALU.mult, inc=[V])
                            pr.op("vector", "tensor_tensor", tmpa[:, :], tmpa[:, :],
                                  rotb[:, :], ALU.add, inc=[V])
                            pr.op("vector", "tensor_tensor", src, tmpa[:, :],
                                  rbuf[:, :], ALU.mult, inc=[V])
                            guard["tmpa"] = (V, V.n)
                    rot_v = V.at()
                if not _RUN["attn"]:
                    continue
                with contextlib.ExitStack() as scL:
                    AL = lambda n, sh, dt=BF16: scL.enter_context(
                        nc.sbuf_tensor(f"{n}_b{b}L{L}", sh, dt))
                    pa4 = AL("pa4", [128, 6 * 1024])
                    pt2 = AL("pt2", [128, 2 * 1024])
                    oTb = AL("oTb", [128, 4 * T])
                    wosb = AL("wosb", [128, 4 * C])
                    pending = []
                    pt_state = {}

                    def flush_prev(prev):
                        cc, w, qi, klo, klen, par, sum_c = prev
                        pr.op("scalar", "activation",
                              statc[:, 8 * par + 4:8 * par + 6],
                              statc[:, 8 * par + 6:8 * par + 8], AF.Ln,
                              wait=[(Cc, sum_c)], inc=[Cc])
                        pr.op("scalar", "activation",
                              statc[:, 8 * par + 4:8 * par + 6],
                              statc[:, 8 * par + 4:8 * par + 6], AF.Exp,
                              scale=negone[:, :], inc=[Cc])
                        rc_c = Cc.at()
                        for hh in range(2):
                            pr.op("vector", "tensor_scalar",
                                  pa4[:, (2 * par + hh) * 1024:
                                      (2 * par + hh) * 1024 + klen],
                                  pa4[:, (2 * par + hh) * 1024:
                                      (2 * par + hh) * 1024 + klen],
                                  statc[:, 8 * par + 4 + hh:8 * par + 5 + hh],
                                  None, ALU.mult, wait=[rc_c], inc=[V])
                        norm_v = V.n
                        nk = klen // 128
                        gpar = par % 2
                        bank = ps_pt if gpar == 0 else ps_b
                        for hh in range(2):
                            for j0 in range(0, nk, 4):
                                jn = min(4, nk - j0)
                                for j in range(j0, j0 + jn):
                                    pr.op("tensor", "matmul",
                                          bank[:, 128 * (j - j0):128 * (j - j0 + 1)],
                                          pa4[:, (2 * par + hh) * 1024 + 128 * j:
                                              (2 * par + hh) * 1024 + 128 * (j + 1)],
                                          idn[:, :], start=True, stop=True,
                                          wait=[(V.sem, norm_v)] + gw(("pt", gpar)) +
                                               (gw("ps_b") if gpar == 1 else []),
                                          inc=[Tc])
                                pr.op("vector", "tensor_copy",
                                      pt2[:, 1024 * hh + 128 * j0:
                                          1024 * hh + 128 * (j0 + jn)],
                                      bank[:, 0:128 * jn], wait=[Tc.at()], inc=[V])
                                guard[("pt", gpar)] = (V, V.n)
                        pt_c = V.at()
                        for hh in range(2):
                            for j in range(nk):
                                tj = (klo + 128 * j) // 128
                                pr.op("tensor", "matmul",
                                      ps_cx[64 * hh:64 * hh + 64, 0:128],
                                      vtm[:, (4 * tj + cc) * 128 + 64 * hh:
                                          (4 * tj + cc) * 128 + 64 * hh + 64],
                                      pt2[:, 1024 * hh + 128 * j:
                                          1024 * hh + 128 * (j + 1)],
                                      start=(j == 0), stop=(j == nk - 1),
                                      tile_position=(0, 64 * hh),
                                      wait=[pt_c] + gw("ps_cx"), inc=[Tc])
                        pr.op("scalar", "copy",
                              oTb[:, cc * T + 512 * w + 128 * qi:
                                  cc * T + 512 * w + 128 * (qi + 1)],
                              ps_cx[:, 0:128], wait=[Tc.at()], inc=[Cc])
                        guard["ps_cx"] = (Cc, Cc.n)
                        pt_state[par] = (Tc, Tc.n)

                    it = 0
                    for cc in range(4):
                        for w in range(4):
                            for qi in range(4):
                                if w == 0:
                                    klo, klen, amask = 0, 128 * (qi + 1), False
                                else:
                                    klo, klen, amask = 512 * (w - 1) + 128 * qi, 640, True
                                q0 = cc * T + 512 * w + 128 * qi
                                par = it % 3
                                it += 1
                                for hh in range(2):
                                    hl = 64 * hh
                                    pst = ps_sc[:, 1024 * hh:1024 * hh + klen]
                                    w1sz = min(512, klen)
                                    pr.op("tensor", "matmul", pst[:, 0:w1sz],
                                          nB[hl:hl + 64, q0:q0 + 128],
                                          nC[hl:hl + 64, cc * T + klo:
                                             cc * T + klo + w1sz],
                                          start=True, stop=False,
                                          tile_position=(hl, 0),
                                          wait=[rot_v] + gw(("sc", hh)) + gw("ps_sc"),
                                          inc=[Tc])
                                    if amask:
                                        pr.op("tensor", "matmul", pst[:, 0:128],
                                              idn[:, :], maska_b[:, :],
                                              start=False, stop=True,
                                              wait=[consts_l[1]], inc=[Tc])
                                        pr.op("tensor", "matmul", pst[:, 512:640],
                                              nB[hl:hl + 64, q0:q0 + 128],
                                              nC[hl:hl + 64, cc * T + klo + 512:
                                                 cc * T + klo + 640],
                                              start=True, stop=False,
                                              tile_position=(hl, 0), inc=[Tc])
                                    pr.op("tensor", "matmul",
                                          pst[:, klen - 128:klen],
                                          idn[:, :], maskc_b[:, :],
                                          start=False, stop=True,
                                          wait=[consts_l[1]], inc=[Tc])
                                qk_t = Tc.at()
                                for hh in range(2):
                                    pr.op("vector", "tensor_reduce",
                                          statc[:, 8 * par + hh:8 * par + hh + 1],
                                          ps_sc[:, 1024 * hh:1024 * hh + klen],
                                          axis=AXX, op=ALU.max, negate=True,
                                          wait=[qk_t] + gw(("st", par)), inc=[V])
                                mask_v = V.at()
                                for hh in range(2):
                                    pr.op("scalar", "activation",
                                          pa4[:, (2 * par + hh) * 1024:
                                              (2 * par + hh) * 1024 + klen],
                                          ps_sc[:, 1024 * hh:1024 * hh + klen],
                                          AF.Exp,
                                          bias=statc[:, 8 * par + hh:8 * par + hh + 1],
                                          scale=onecol_f[:, :],
                                          accum_out=statc[:, 8 * par + 6 + hh:
                                                          8 * par + 7 + hh],
                                          wait=[mask_v] + ([pt_state[par]]
                                                           if par in pt_state else []),
                                          inc=[Cc])
                                guard["ps_sc"] = (Cc, Cc.n)
                                guard[("st", par)] = (Cc, Cc.n)
                                pending.append((cc, w, qi, klo, klen, par, Cc.n))
                                if len(pending) > 2:
                                    flush_prev(pending.pop(0))
                    for p_ in pending:
                        flush_prev(p_)
                    loc_v = Cc.at()
                    for k in range(4):
                        pr.dma("gpsimd", wosb[:, C * k:C * (k + 1)], d_wo4[L, k],
                               LG, wait=pg())
                    wo_l = LG.at()
                    for oc in range(NCH):
                        for n in range(4):
                            for k in range(4):
                                pr.op("tensor", "matmul", ps_a[:, :],
                                      wosb[:, C * k + 128 * oc:C * k + 128 * (oc + 1)],
                                      oTb[:, k * T + 512 * n:k * T + 512 * (n + 1)],
                                      start=(k == 0), stop=(k == 3),
                                      wait=[wo_l, loc_v] + gw("ps_a"), inc=[Tc])
                            pr.op("vector", "scalar_tensor_tensor",
                                  chunk(xb, oc, 512 * n, 512), ps_a[:, :],
                                  onecol_f[:, :], chunk(xb, oc, 512 * n, 512),
                                  ALU.mult, ALU.add, wait=[Tc.at()], inc=[V])
                            guard["ps_a"] = (V, V.n)
                    x1a_v = V.at()
                    if "x1a" in dbg and b == 0 and L == 0:
                        dbg_outs["x1a"] = DO("dbg_x1a", [128, NCH * T])
                        pr.dma("sync", dbg_outs["x1a"][:, :], xb[:, :], SO,
                               wait=[x1a_v])
                    if "x1a1" in dbg and b == 0 and L == 1:
                        dbg_outs["x1a1"] = DO("dbg_x1a1", [128, NCH * T])
                        pr.dma("sync", dbg_outs["x1a1"][:, :], xb[:, :], SO,
                               wait=[x1a_v])

            if not _RUN["mlp"]:
                continue
            with contextlib.ExitStack() as scM:
                AM = lambda n, sh, dt=BF16: scM.enter_context(
                    nc.sbuf_tensor(f"{n}_b{b}L{L}", sh, dt))
                stM = mk_stats(scM, f"_M{b}{L}")
                mb2 = AM("mb2", [128, NCH * 1024])
                nn2 = AM("nn2", [128, NCH * 1024])
                h1b = AM("h1b", [128, NJ * 1024])
                w1s = [AM(f"w1s{i}", [128, 2 * 1024]) for i in range(2)]
                w2s = [AM(f"w2s{i}", [128, NJ * 128]) for i in range(2)]
                gsc = AM("gsc", [128, 512])
                sv = scM.enter_context(nc.sbuf_tensor(f"sv_b{b}L{L}", [128, 4], F32))
                pr.op("vector", "tensor_copy",
                      sv[:, 0:4].rearrange("p (c o) -> p c o", o=1),
                      xb[:, :].rearrange("p (c t) -> p c t", t=T)[:, 4:8, 1023:1024],
                      wait=pg(), inc=[V])
                pr.bar("vector")
                for half in range(2):
                    t0 = 1024 * half
                    nh_v = ln_block(stM, xb, True, mb2, 1024, nn2, 1024,
                                    lambda ch: lgcS[:, NCH * L + ch:NCH * L + ch + 1],
                                    lambda ch: lbcS[:, NCH * L + ch:NCH * L + ch + 1],
                                    t0, 1024, wait0=pg(),
                                    edge=(lambda ch: sv[:, ch - 4:ch - 3])
                                         if half == 1 else None)
                    for j in range(NJ):
                        par = j % 2
                        pr.dma("gpsimd", w1s[par][:, 0:1024], d_w1a[L, j], LG,
                               wait=(pg() if (j < 2 and half == 0) else
                                     gw(("w1", par))))
                        pr.dma("gpsimd", w1s[par][:, 1024:2048], d_w1g[L, j], LG)
                        w1_l = LG.at()
                        for n2 in range(2):
                            for k in range(NCH):
                                pr.op("tensor", "matmul", ps_a[:, :],
                                      w1s[par][:, 128 * k:128 * (k + 1)],
                                      nn2[:, k * 1024 + 512 * n2:k * 1024 + 512 * (n2 + 1)],
                                      start=(k == 0), stop=(k == NCH - 1),
                                      wait=[w1_l, nh_v] + gw("ps_a"), inc=[Tc])
                            for k in range(NCH):
                                pr.op("tensor", "matmul", ps_b[:, :],
                                      w1s[par][:, 1024 + 128 * k:1024 + 128 * (k + 1)],
                                      nn2[:, k * 1024 + 512 * n2:k * 1024 + 512 * (n2 + 1)],
                                      start=(k == 0), stop=(k == NCH - 1),
                                      wait=[w1_l] + gw("ps_b"), inc=[Tc])
                            mm_t = Tc.at()
                            pr.op("scalar", "activation", gsc[:, :], ps_b[:, :],
                                  AF.Gelu, bias=b1gcS[:, NJ * L + j:NJ * L + j + 1],
                                  scale=1.0, wait=[mm_t] + gw("gsc"), inc=[Cc])
                            guard["ps_b"] = (Cc, Cc.n)
                            pr.op("vector", "scalar_tensor_tensor",
                                  h1b[:, 1024 * j + 512 * n2:1024 * j + 512 * (n2 + 1)],
                                  ps_a[:, :], b1acS[:, NJ * L + j:NJ * L + j + 1],
                                  gsc[:, :], ALU.add, ALU.mult,
                                  wait=[Cc.at()], inc=[V])
                            guard["ps_a"] = (V, V.n)
                            guard["gsc"] = (V, V.n)
                        guard[("w1", par)] = (Tc, Tc.n)
                    h1_v = V.at()
                    for oc in range(NCH):
                        par = oc % 2
                        pr.dma("gpsimd", w2s[par][:, :], d_w2b[L, oc], LG,
                               wait=(pg() if (oc < 2 and half == 0) else
                                     gw(("w2", par))))
                        w2_l = LG.at()
                        for n2 in range(2):
                            for k in range(NJ):
                                pr.op("tensor", "matmul", ps_a[:, :],
                                      w2s[par][:, 128 * k:128 * (k + 1)],
                                      h1b[:, 1024 * k + 512 * n2:1024 * k + 512 * (n2 + 1)],
                                      start=(k == 0), stop=(k == NJ - 1),
                                      wait=[w2_l, h1_v] + gw("ps_a"), inc=[Tc])
                            pr.op("vector", "scalar_tensor_tensor",
                                  chunk(xb, oc, t0 + 512 * n2, 512), ps_a[:, :],
                                  b2cS[:, NCH * L + oc:NCH * L + oc + 1],
                                  chunk(xb, oc, t0 + 512 * n2, 512), ALU.add, ALU.add,
                                  wait=[Tc.at()], inc=[V])
                            guard["ps_a"] = (V, V.n)
                        guard[("w2", par)] = (Tc, Tc.n)
                x1_v = V.at()
                if "x1L" in dbg and b == 0 and L == 0:
                    dbg_outs["x1L"] = DO("dbg_x1L", [128, NCH * T])
                    pr.dma("sync", dbg_outs["x1L"][:, :], xb[:, :], SO, wait=[x1_v])
                if "x1L1" in dbg and b == 0 and L == 1:
                    dbg_outs["x1L1"] = DO("dbg_x1L1", [128, NCH * T])
                    pr.dma("sync", dbg_outs["x1L1"][:, :], xb[:, :], SO, wait=[x1_v])

        if not _RUN["final"]:
            continue
        with contextlib.ExitStack() as scF:
            AFs = lambda n, sh: scF.enter_context(
                nc.sbuf_tensor(f"{n}_b{b}", sh, F32))
            xspb = [AFs(f"xsp{i}", [128, T]) for i in range(2)]
            obufb = [AFs(f"obuf{i}", [128, T]) for i in range(2)]
            PG = pg()
            for ch in range(NCH):
                par = ch % 2
                pr.dma("sync", xspb[par][:, :], spill[ch],
                       LS, wait=PG + [spill_s] + gw(("fin", par)))
                pr.op("vector", "scalar_tensor_tensor", obufb[par][:, :],
                      chunk(xb, ch), mcol(5, b, ch), xspb[par][:, :],
                      ALU.mult, ALU.add, wait=[LS.at(), x1_v], inc=[V])
                pr.dma("sync", outT[b, ch], obufb[par][:, :], SO, wait=[V.at()])
                guard[("fin", par)] = SO.at()
        guard["xb"] = SO.at()
        guard["spill"] = LS.at()

    pr.emit("sync", lambda e: e.nop(), wait=[SO.at(), SG.at()])

    with nc.Block() as block:
        pr.replay(block)
    es.close()
    return nc, dbg_outs


_DBG = ()


def kernel(**inputs):
    in_maps = _stage(inputs)
    nc, _ = build_program(_DBG)
    res = run_bass_kernel_spmd(nc, in_maps, core_ids=list(range(NCORES)))
    out = np.empty((16, T, C), np.float32)
    for core in range(NCORES):
        o = np.asarray(res.results[core]["outT"])
        out[core * BPC:(core + 1) * BPC] = (
            o.reshape(BPC, C, T).transpose(0, 2, 1))
    return out


# version 13
# speedup vs baseline: 1.0059x; 1.0059x over previous
import contextlib
import numpy as np

import concourse.bass as bass
import concourse.mybir as mybir
from concourse.bass_utils import run_bass_kernel_spmd

F32 = mybir.dt.float32
BF16 = mybir.dt.bfloat16
AF = mybir.ActivationFunctionType
ALU = mybir.AluOpType
AXX = mybir.AxisListType.X

NCORES = 8
BPC = 2
T = 2048
C = 1024
NCH = 8
NH = 16
WIN = 512
HID = 2730
HIDP = 2816
NJ = 22
NEG = -1.0e30


_RUN = dict(mod=True, modcols=True, aw=True, batches=2, loads=True, p1=True, p3=True, p4=True, layers=2, attn=True, mlp=True, final=True)

ENGINES = ("tensor", "vector", "scalar", "gpsimd", "sync")


class _Ctr:

    def __init__(self, sem, name):
        self.sem, self.n, self.name = sem, 0, name

    def at(self, value=None):
        return (self.sem, self.n if value is None else value)


class _DmaCtr:

    def __init__(self, sems, name):
        self.sems = sems
        self.counts = [0] * len(sems)
        self.rr = 0
        self.name = name

    def issue(self):
        i = self.rr % len(self.sems)
        self.rr += 1
        selfwait = (self.sems[i], self.counts[i])
        self.counts[i] += 16
        return selfwait, (self.sems[i], 16), (self.sems[i], self.counts[i])

    def at(self):
        return [(s, c) for s, c in zip(self.sems, self.counts) if c > 0]


def _expand(w, out):
    if isinstance(w, list):
        for x in w:
            _expand(x, out)
    else:
        out.append(w)


class _Prog:
    def __init__(self, nc, es):
        self.nc, self.es = nc, es
        self.streams = {e: [] for e in ENGINES}
        self._lw = {}
        self._dma_lw = {}
        self.auto = {}

    def ctr(self, name):
        sem = self.es.enter_context(self.nc.semaphore(name))
        return _Ctr(sem, name)

    def dctr(self, name, k):
        sems = [self.es.enter_context(self.nc.semaphore(f"{name}{i}"))
                for i in range(k)]
        return _DmaCtr(sems, name)

    def emit(self, eng, fn, wait=(), inc=(), raw_incs=(), nodedup=False):
        flat = []
        _expand(list(wait), flat)
        waits = []
        dedup = not nodedup
        for item in flat:
            sem, value = item
            if isinstance(sem, _Ctr):
                sem = sem.sem
            if value > 0 and (not dedup or self._lw.get((eng, id(sem)), -1) < value):
                if (sem, value) not in waits:
                    waits.append((sem, value))
                self._lw[(eng, id(sem))] = value
        incs = list(raw_incs)
        for c in inc:
            c.n += 1
            incs.append((c.sem, 1))
        self.streams[eng].append((waits, fn, incs))

    def op(self, eng, method, *args, wait=(), inc=(), lowp=False, **kwargs):
        ac = self.auto.get(eng)
        if ac is not None:
            wait = list(wait) + [ac.at()]
            if ac not in inc:
                inc = list(inc) + [ac]
        if lowp:
            def fn(e, m=method, a=args, k=kwargs, nc=self.nc):
                with nc.allow_low_precision(reason="bf16 stat tables"):
                    return getattr(e, m)(*a, **k)
        else:
            fn = lambda e, m=method, a=args, k=kwargs: getattr(e, m)(*a, **k)
        self.emit(eng, fn, wait=wait, inc=inc)

    def dma(self, eng, out_ap, in_ap, ctr, wait=(), **kw):
        selfwait, incspec, done = ctr.issue()
        flat = []
        _expand(list(wait), flat)
        key = id(selfwait[0])
        seen = self._dma_lw.setdefault(key, {})
        out = []
        for sem, value in flat:
            if isinstance(sem, _Ctr):
                sem = sem.sem
            if value > 0 and seen.get(id(sem), -1) < value:
                out.append((sem, value))
                seen[id(sem)] = value
        self.emit(eng, lambda e, o=out_ap, i=in_ap, k=kw: e.dma_start(o, i, **k),
                  wait=out + [selfwait], raw_incs=[incspec], nodedup=True)
        return [done]

    def bar(self, eng, wait=(), inc=()):
        ac = self.auto.get(eng)
        if ac is not None:
            wait = list(wait) + [ac.at()]
            if ac not in inc:
                inc = list(inc) + [ac]
        self.emit(eng, lambda e: e.drain(), wait=wait, inc=inc)

    def replay(self, block):
        for eng_name in ENGINES:
            stream = self.streams[eng_name]
            if not stream:
                continue

            def body(e, stream=stream):
                for waits, fn, incs in stream:
                    for sem, value in waits[:-1]:
                        e.wait_ge(sem, value)
                    instr = fn(e)
                    if waits:
                        instr._wait_ge(waits[-1][0], waits[-1][1])
                    for sem, k in incs:
                        instr.then_inc(sem, k)

            getattr(block, eng_name)(body)


def _rot_tables():
    d = np.arange(64)
    j = d % 32
    sign = np.where(d % 64 < 32, -1.0, 1.0)
    half_idx = (2 * j).astype(np.float64)
    inv_freq = (np.float32(1.0) / np.float32(10000.0) ** (half_idx.astype(np.float32) / np.float32(64.0))).astype(np.float64)
    t = np.arange(T, dtype=np.float64)
    freqs = t[None, :] * inv_freq[:, None]
    cos, sin = np.cos(freqs), np.sin(freqs)
    base = (half_idx + 0.4 * 64.0) / (1.4 * 64.0)
    power = (t - T // 2) / float(WIN // 2)
    scale = base[:, None] ** power[None, :]
    t1q = cos * scale
    t2q = sign[:, None] * sin * scale
    t1k = cos / scale
    t2k = sign[:, None] * sin / scale
    out = []
    for tb in (t1q, t2q, t1k, t2k):
        tb2 = np.concatenate([tb, tb], 0)
        hi = tb2.astype(np.float32)
        lo = (tb2 - hi.astype(np.float64)).astype(np.float32)
        out.append(np.stack([hi, lo], 0))
    return np.stack(out, 0)


def _consts():
    t_i = np.arange(128)[:, None]
    s_i = np.arange(128)[None, :]
    mask_c = np.where(s_i <= t_i, 0.0, NEG).astype(np.float32)
    mask_a = np.where(t_i <= s_i, 0.0, NEG).astype(np.float32)
    iden = np.eye(128, dtype=np.float32)
    e2 = np.zeros((128, 2), np.float32)
    e2[0:64, 0] = 1.0
    e2[64:128, 1] = 1.0
    e2t = np.zeros((2, 128), np.float32)
    e2t[0, 0:64] = 1.0
    e2t[1, 64:128] = 1.0
    return mask_c, mask_a, iden, e2, e2t


def _stage(inputs):
    f = lambda a: np.ascontiguousarray(np.asarray(a), dtype=np.float32)
    x = f(inputs["L0_attn_x"])
    wdqT = np.ascontiguousarray(f(inputs["L0_attn_W_dq"]).T).reshape(NCH, 128, 128)
    wdkvT = np.ascontiguousarray(f(inputs["L0_attn_W_dkv"]).T).reshape(NCH, 128, 128)
    wuq = f(inputs["L0_attn_W_uq"]).reshape(NCH, 128, 128)
    wuk = f(inputs["L0_attn_W_uk"]).reshape(NCH, 128, 128)
    wuv = f(inputs["L0_attn_W_uv"]).reshape(NCH, 128, 128)
    wdq = f(inputs["L0_attn_W_dq"])
    woT = np.ascontiguousarray(f(inputs["L0_attn_W_o"]).T).reshape(NCH, 128, C)
    wqkv = f(inputs["L0_ffn_wqkv"])
    wqkv3 = np.ascontiguousarray(
        wqkv.reshape(2, NCH, 128, 3, 512).transpose(0, 3, 1, 2, 4))
    wo = f(inputs["L0_ffn_wo"]).reshape(2, 4, 128, C)
    w1 = f(inputs["L0_ffn_w1"])
    w1p = np.zeros((2, C, 2 * HIDP), np.float32)
    w1p[:, :, 0:HID] = w1[:, :, 0:HID]
    w1p[:, :, HIDP:HIDP + HID] = w1[:, :, HID:]
    w1a = np.ascontiguousarray(
        w1p[:, :, 0:HIDP].reshape(2, NCH, 128, NJ, 128).transpose(0, 3, 2, 1, 4))
    w1g = np.ascontiguousarray(
        w1p[:, :, HIDP:].reshape(2, NCH, 128, NJ, 128).transpose(0, 3, 2, 1, 4))
    w2 = f(inputs["L0_ffn_w2"])
    w2p = np.zeros((2, HIDP, C), np.float32)
    w2p[:, 0:HID, :] = w2
    w2b = np.ascontiguousarray(
        w2p.reshape(2, NJ, 128, NCH, 128).transpose(0, 3, 2, 1, 4))
    b1 = f(inputs["L0_ffn_b1"])
    b1p = np.zeros((2, 2 * HIDP), np.float32)
    b1p[:, 0:HID] = b1[:, 0:HID]
    b1p[:, HIDP:HIDP + HID] = b1[:, HID:]
    b1ac = np.ascontiguousarray(b1p[:, 0:HIDP].reshape(2, NJ, 128).transpose(0, 2, 1))
    b1gc = np.ascontiguousarray(b1p[:, HIDP:].reshape(2, NJ, 128).transpose(0, 2, 1))
    rc = lambda a: np.ascontiguousarray(f(a).reshape(2, NCH, 128).transpose(0, 2, 1))
    b2c = rc(inputs["L0_ffn_b2"])
    lgc = rc(inputs["L0_ffn_ln_g"])
    lbc = rc(inputs["L0_ffn_ln_b"])
    qs = f(inputs["L0_ffn_q_scale"])
    ks = f(inputs["L0_ffn_k_scale"])
    p64 = np.arange(128) % 64
    qsc = np.ascontiguousarray(
        np.stack([qs[:, p64], qs[:, p64 ^ 32], ks[:, p64], ks[:, p64 ^ 32]], 2))
    cfull = f(inputs["L0_c"])
    wmodh = f(inputs["L0_wmod"]).reshape(NCH, 128, 6 * C)
    tabs = _rot_tables()
    mask_c, mask_a, iden, e2, e2t = _consts()
    shared = dict(wdqT=wdqT, wuq=wuq, wdkvT=wdkvT, wuk=wuk, wuv=wuv, wdq=wdq,
                  woT=woT, wqkv3=wqkv3, wo4=wo, w1a=w1a, w1g=w1g, w2b=w2b,
                  b1ac=b1ac, b1gc=b1gc, b2c=b2c, lgc=lgc, lbc=lbc, qsc=qsc,
                  wmodh=wmodh, tabs=tabs, maskc=mask_c, maska=mask_a,
                  idenh=iden, e2h=e2, e2th=e2t)
    in_maps = []
    for core in range(NCORES):
        bsl = slice(core * BPC, (core + 1) * BPC)
        m = dict(shared)
        m["xT"] = np.ascontiguousarray(
            x[bsl].transpose(0, 2, 1)).reshape(BPC, NCH, 128, T)
        m["cmodh"] = np.ascontiguousarray(
            cfull[bsl].reshape(BPC, NCH, 128).transpose(2, 1, 0).reshape(128, 16))
        in_maps.append(m)
    return in_maps


def build_program(dbg=()):
    nc = bass.Bass("TRN2", target_bir_lowering=False)
    es = contextlib.ExitStack()
    pr = _Prog(nc, es)
    DI = lambda n, s: nc.dram_tensor(n, s, F32, kind="ExternalInput")
    DO = lambda n, s: nc.dram_tensor(n, s, F32, kind="ExternalOutput")
    DS = lambda n, s: nc.dram_tensor(n, s, F32, kind="Internal")
    SB = lambda n, s, dt=F32: es.enter_context(nc.sbuf_tensor(n, s, dt))
    PS = lambda n, s, dt=F32: es.enter_context(nc.psum_tensor(n, s, dt))

    xT = DI("xT", [BPC, NCH, 128, T])
    d_wdqT = DI("wdqT", [NCH, 128, 128]); d_wuq = DI("wuq", [NCH, 128, 128])
    d_wdkvT = DI("wdkvT", [NCH, 128, 128]); d_wuk = DI("wuk", [NCH, 128, 128])
    d_wuv = DI("wuv", [NCH, 128, 128]); d_wdq = DI("wdq", [128, C])
    d_woT = DI("woT", [NCH, 128, C])
    d_wqkv3 = DI("wqkv3", [2, 3, NCH, 128, 512])
    d_wo4 = DI("wo4", [2, 4, 128, C])
    d_w1a = DI("w1a", [2, NJ, 128, NCH, 128]); d_w1g = DI("w1g", [2, NJ, 128, NCH, 128])
    d_w2b = DI("w2b", [2, NCH, 128, NJ, 128])
    d_b1ac = DI("b1ac", [2, 128, NJ]); d_b1gc = DI("b1gc", [2, 128, NJ])
    d_b2c = DI("b2c", [2, 128, NCH]); d_lgc = DI("lgc", [2, 128, NCH])
    d_lbc = DI("lbc", [2, 128, NCH]); d_qsc = DI("qsc", [2, 128, 4])
    d_cmod = DI("cmodh", [128, 16]); d_wmod = DI("wmodh", [NCH, 128, 6 * C])
    d_tabs = DI("tabs", [4, 2, 128, T])
    d_maskc = DI("maskc", [128, 128]); d_maska = DI("maska", [128, 128])
    d_iden = DI("idenh", [128, 128]); d_e2 = DI("e2h", [128, 2]); d_e2t = DI("e2th", [2, 128])
    outT = DO("outT", [BPC, NCH, 128, T])
    spill = DS("spill", [NCH, 128, T])
    modscr = DS("modscr", [2, 6 * C])
    asbscr = DS("asbscr", [128, NH * 128])
    hsbscr = DS("hsbscr", [128, C])
    dbg_outs = {}

    LG = pr.dctr("LG", 8)
    LS = pr.dctr("LS", 8)
    SO = pr.dctr("SO", 4)
    SG = pr.dctr("SG", 2)
    Tc = pr.ctr("T"); V = pr.ctr("V"); Cc = pr.ctr("C")
    pr.auto["vector"] = V
    pr.auto["scalar"] = Cc

    maskc = SB("s_maskc", [128, 128]); maska = SB("s_maska", [128, 128])
    maskc_b = SB("s_maskcb", [128, 128], BF16); maska_b = SB("s_maskab", [128, 128], BF16)
    idn = SB("s_idn", [128, 128], BF16)
    e2 = SB("s_e2", [128, 2], BF16); e2t = SB("s_e2t", [2, 128], BF16)
    ones1 = SB("s_ones1", [1, 128], BF16)
    onescol = SB("s_onescol", [128, 1], BF16)
    onecol_f = SB("s_onecol_f", [128, 1])
    i1024 = SB("s_i1024", [128, 1])
    i64 = SB("s_i64", [128, 1])
    epscol = SB("s_epscol", [128, 1])
    negone = SB("s_negone", [128, 1])
    modc = SB("s_modc", [128, 96])
    qscol = SB("s_qscol", [128, 8])
    b1acS = SB("s_b1ac", [128, 2 * NJ]); b1gcS = SB("s_b1gc", [128, 2 * NJ])
    b2cS = SB("s_b2c", [128, 2 * NCH]); lgcS = SB("s_lgc", [128, 2 * NCH])
    lbcS = SB("s_lbc", [128, 2 * NCH])
    statc = SB("s_statc", [128, 32])
    diag2 = SB("s_diag2", [128, 512], BF16)
    xb = SB("s_xb", [128, NCH * T])

    def mcol(v, b, ch):
        return modc[:, v * 16 + b * 8 + ch: v * 16 + b * 8 + ch + 1]

    pr.dma("sync", maskc[:, :], d_maskc[:, :], LS)
    pr.dma("sync", maska[:, :], d_maska[:, :], LS)
    pr.dma("gpsimd", idn[:, :], d_iden[:, :], LG)
    pr.dma("gpsimd", maskc_b[:, :], d_maskc[:, :], LG)
    pr.dma("gpsimd", maska_b[:, :], d_maska[:, :], LG)
    pr.dma("gpsimd", e2[:, :], d_e2[:, :], LG)
    pr.dma("gpsimd", e2t[:, :], d_e2t[:, :], LG)
    for L in range(2):
        pr.dma("sync", qscol[:, 4 * L:4 * L + 4], d_qsc[L], LS)
        pr.dma("sync", b1acS[:, NJ * L:NJ * (L + 1)], d_b1ac[L], LS)
        pr.dma("sync", b1gcS[:, NJ * L:NJ * (L + 1)], d_b1gc[L], LS)
        pr.dma("sync", b2cS[:, NCH * L:NCH * (L + 1)], d_b2c[L], LS)
        pr.dma("sync", lgcS[:, NCH * L:NCH * (L + 1)], d_lgc[L], LS)
        pr.dma("sync", lbcS[:, NCH * L:NCH * (L + 1)], d_lbc[L], LS)
    pr.op("vector", "memset", ones1[:, :], 1.0, inc=[V])
    pr.op("vector", "memset", onescol[:, :], 1.0, inc=[V])
    pr.op("vector", "memset", onecol_f[:, :], 1.0, inc=[V])
    pr.op("vector", "memset", i1024[:, :], 1.0 / 1024.0, inc=[V])
    pr.op("vector", "memset", i64[:, :], 1.0 / 64.0, inc=[V])
    pr.op("vector", "memset", epscol[:, :], 1.0e-5, inc=[V])
    pr.op("vector", "memset", negone[:, :], -1.0, inc=[V])
    consts_v = V.at()
    consts_l = (LS.at(), LG.at())

    if _RUN["mod"]:
      with contextlib.ExitStack() as st0:
        SBs = lambda n, s, dt=F32: st0.enter_context(nc.sbuf_tensor(n, s, dt))
        PSs = lambda n, s: st0.enter_context(nc.psum_tensor(n, s, F32))
        cmods = SBs("u_cmods", [128, 16])
        cmodf = SBs("u_cmodf", [128, 16])
        wms = [SBs(f"u_wms{i}", [128, 6 * C]) for i in range(2)]
        modrow = SBs("u_modrow", [2, 6 * C])
        mscr = SBs("u_mscr", [2, C])
        psmod = [PSs(f"u_psmod{i}", [2, 512]) for i in range(2)]
        pr.dma("sync", cmodf[:, :], d_cmod[:, :], LS)
        pr.op("scalar", "activation", cmods[:, :], cmodf[:, :], AF.Silu,
              wait=[LS.at()], inc=[Cc])
        pr.op("vector", "memset", modrow[:, :], 0.0, inc=[V])
        silu_c = Cc.at()
        mrow_v = V.at()
        mm_done, ev_done, ld_done = {}, {}, {}
        for k in range(NCH):
            par = k % 2
            w = [consts_v] + ([(Tc.sem, mm_done[par])] if par in mm_done else [])
            ld_done[par] = pr.dma("sync", wms[par][:, :], d_wmod[k], LS, wait=w)
            for n in range(12):
                ppar = n % 2
                w = [silu_c] + ld_done[par]
                if ppar in ev_done:
                    w.append((V.sem, ev_done[ppar]))
                pr.op("tensor", "matmul", psmod[ppar][:, :],
                      cmods[:, 2 * k:2 * k + 2], wms[par][:, 512 * n:512 * (n + 1)],
                      start=True, stop=True, wait=w, inc=[Tc])
                pr.op("vector", "tensor_tensor", modrow[:, 512 * n:512 * (n + 1)],
                      modrow[:, 512 * n:512 * (n + 1)], psmod[ppar][:, :], ALU.add,
                      wait=[Tc.at(), mrow_v], inc=[V])
                ev_done[ppar] = V.n
            mm_done[par] = Tc.n
        for lo in (2 * C, 5 * C):
            seg = modrow[:, lo:lo + C]
            pr.op("vector", "tensor_single_scalar", mscr[:, :], seg, 0.0, ALU.is_ge,
                  inc=[V])
            pr.bar("vector")
            pr.op("vector", "tensor_scalar", mscr[:, :], mscr[:, :], 2.0, -1.0,
                  ALU.mult, ALU.add, inc=[V])
            pr.op("scalar", "activation", seg, seg, AF.Abs, wait=[V.at()], inc=[Cc])
            pr.op("vector", "tensor_scalar", seg, seg, 0.25, None, ALU.max,
                  wait=[Cc.at()], inc=[V])
            pr.bar("vector")
            pr.op("vector", "tensor_tensor", seg, seg, mscr[:, :], ALU.mult, inc=[V])
        pr.dma("sync", modscr[:, :], modrow[:, :], SO, wait=[V.at()])
        mod_st = SO.at()
        for v in range(6 if _RUN["modcols"] else 0):
            for b in range(BPC):
                for ch in range(NCH):
                    pr.dma("sync", mcol(v, b, ch),
                           modscr[b:b + 1, v * C + ch * 128: v * C + (ch + 1) * 128]
                           .rearrange("o p -> p o"), LS,
                           wait=[mod_st], allow_slow_non_contiguous=True)
        pr.op("vector", "tensor_scalar", modc[:, 16:32], modc[:, 16:32], 1.0, None,
              ALU.add, wait=[LS.at()], inc=[V])
        pr.op("vector", "tensor_scalar", modc[:, 64:80], modc[:, 64:80], 1.0, None,
              ALU.add, inc=[V])
    mod_ready = V.at()

    asb_st = None
    if _RUN["aw"]:
      with contextlib.ExitStack() as st1:
        SBs = lambda n, s, dt=BF16: st1.enter_context(nc.sbuf_tensor(n, s, dt))
        PSs = lambda n, s: st1.enter_context(nc.psum_tensor(n, s, F32))
        wuq_b = SBs("u_wuq", [128, C]); wuk_b = SBs("u_wuk", [128, C])
        wuv_b = SBs("u_wuv", [128, C]); wdq_b = SBs("u_wdq", [128, C])
        wotb = [SBs(f"u_wot{i}", [128, C]) for i in range(2)]
        m1b = SBs("u_m1b", [128, 128]); gsb = SBs("u_gsb", [128, C])
        asb = SBs("u_asb", [128, NH * 128]); hsb = SBs("u_hsb", [128, C])
        psa = PSs("u_psa", [128, 128])
        psh = PSs("u_psh", [128, C])
        aw_pg = [Tc.at(), V.at(), Cc.at(), SO.at(), LS.at()]
        for ch in range(NCH):
            pr.dma("gpsimd", wuq_b[:, 128 * ch:128 * (ch + 1)], d_wuq[ch], LG,
                   wait=aw_pg)
            pr.dma("gpsimd", wuk_b[:, 128 * ch:128 * (ch + 1)], d_wuk[ch], LG)
            pr.dma("gpsimd", wuv_b[:, 128 * ch:128 * (ch + 1)], d_wuv[ch], LG)
        pr.dma("gpsimd", wdq_b[:, :], d_wdq[:, :], LG)
        aw_l = LG.at()
        for ch in range(NCH):
            pr.op("tensor", "matmul", psa[:, :], wuq_b[:, 128 * ch:128 * (ch + 1)],
                  wuk_b[:, 128 * ch:128 * (ch + 1)], start=(ch == 0),
                  stop=(ch == NCH - 1), wait=[aw_l], inc=[Tc])
        pr.op("vector", "tensor_copy", m1b[:, :], psa[:, :], wait=[Tc.at()], inc=[V])
        for ch in range(NCH):
            pr.op("tensor", "matmul", psa[:, :], wdq_b[:, 128 * ch:128 * (ch + 1)],
                  m1b[:, :], start=True, stop=True, wait=[V.at()], inc=[Tc])
            pr.op("vector", "tensor_copy", gsb[:, 128 * ch:128 * (ch + 1)], psa[:, :],
                  wait=[Tc.at()], inc=[V])
        for h in range(NH):
            hl = 64 * (h % 2)
            ch = h // 2
            pr.op("tensor", "matmul", psa[:, :],
                  wuq_b[hl:hl + 64, 128 * ch:128 * (ch + 1)],
                  gsb[hl:hl + 64, 128 * ch:128 * (ch + 1)],
                  start=True, stop=True, tile_position=(hl, 0),
                  wait=[V.at()], inc=[Tc])
            pr.op("vector", "tensor_scalar", asb[:, 128 * h:128 * (h + 1)], psa[:, :],
                  0.125, None, ALU.mult, wait=[Tc.at()], inc=[V])
        for ch in range(NCH):
            par = ch % 2
            w = [(Tc, Tc.n)] if ch >= 2 else []
            pr.dma("gpsimd", wotb[par][:, :], d_woT[ch], LG, wait=w)
            for n2 in range(2):
                pr.op("tensor", "matmul", psh[:, 512 * n2:512 * (n2 + 1)],
                      wuv_b[:, 128 * ch:128 * (ch + 1)],
                      wotb[par][:, 512 * n2:512 * (n2 + 1)],
                      start=(ch == 0), stop=(ch == NCH - 1),
                      wait=[LG.at()], inc=[Tc])
        pr.op("vector", "tensor_copy", hsb[:, :], psh[:, :], wait=[Tc.at()], inc=[V])
        pr.dma("gpsimd", asbscr[:, :], asb[:, :], SG, wait=[V.at()])
        pr.dma("gpsimd", hsbscr[:, :], hsb[:, :], SG)
        asb_st = SG.at()
        if "asb" in dbg:
            dbg_outs["asb"] = DO("dbg_asb", [128, NH * 128])
            pr.dma("gpsimd", dbg_outs["asb"][:, :], asb[:, :], SG, wait=[V.at()])
            dbg_outs["hsb"] = DO("dbg_hsb", [128, C])
            pr.dma("gpsimd", dbg_outs["hsb"][:, :], hsb[:, :], SG, wait=[V.at()])
    if "modc" in dbg:
        dbg_outs["modc"] = DO("dbg_modc", [128, 96])
        pr.dma("sync", dbg_outs["modc"][:, :], modc[:, :], SO, wait=[mod_ready])

    ps_sc = PS("ps_sc", [128, T])
    ps_a = PS("ps_a", [128, 512])
    ps_b = PS("ps_b", [128, 512])
    ps_pt = PS("ps_pt", [128, 512])
    ps_cx = PS("ps_cx", [128, 512])
    ps_r1 = ps_pt[0:1, 0:512]
    ps_s2 = ps_pt[0:2, 0:512]

    guard = {}

    def gw(key):
        return [guard[key]] if key in guard else []

    def pg():
        return [Tc.at(), V.at(), Cc.at(), LS.at(), LG.at(), SO.at(), SG.at()]

    def chunk(buf, ch, t0=0, tl=T):
        return buf[:, ch * T + t0: ch * T + t0 + tl]

    def ln_block(st, src, shifted, cast_to, cast_stride, out_to, out_stride,
                 gcol, bcol, t0, tl, wait0=(), norm_wait=(), edge=None):
        nt = tl // 512
        RSTD, MURS = st["RSTD"], st["MURS"]
        srow, qrow, sqb = st["srow"], st["qrow"], st["sqb"]
        muf, sqf, m2f, tmpf = st["muf"], st["sqf"], st["m2f"], st["tmpf"]
        for ch in range(NCH):
            dst = cast_to[:, ch * cast_stride: ch * cast_stride + tl]
            if shifted and ch >= 4:
                if t0 == 0:
                    pr.op("vector", "memset", dst[:, 0:1], 0.0, wait=list(wait0), inc=[V])
                    pr.op("vector", "tensor_copy", dst[:, 1:tl],
                          src[:, ch * T: ch * T + tl - 1], inc=[V])
                elif edge is not None:
                    pr.op("vector", "tensor_copy", dst[:, 0:1], edge(ch),
                          wait=list(wait0), inc=[V])
                    pr.op("vector", "tensor_copy", dst[:, 1:tl],
                          src[:, ch * T + t0: ch * T + t0 + tl - 1], inc=[V])
                else:
                    pr.op("vector", "tensor_copy", dst,
                          src[:, ch * T + t0 - 1: ch * T + t0 + tl - 1],
                          wait=list(wait0), inc=[V])
            else:
                pr.op("vector", "tensor_copy", dst,
                      src[:, ch * T + t0: ch * T + t0 + tl], wait=list(wait0), inc=[V])
        cast_v = V.at()
        for n in range(nt):
            for ch in range(NCH):
                rhs = cast_to[:, ch * cast_stride + n * 512: ch * cast_stride + (n + 1) * 512]
                pr.op("tensor", "matmul", ps_r1, onescol[:, :], rhs,
                      start=(ch == 0), stop=(ch == NCH - 1),
                      wait=[cast_v, consts_v] + gw("ps_pt"), inc=[Tc])
            pr.op("vector", "tensor_copy", srow[:, n * 512:(n + 1) * 512], ps_r1,
                  wait=[Tc.at()], inc=[V])
            guard["ps_pt"] = (V, V.n)
            for ch in range(NCH):
                rhs = cast_to[:, ch * cast_stride + n * 512: ch * cast_stride + (n + 1) * 512]
                pr.op("scalar", "activation", sqb[:, :], rhs, AF.Square,
                      wait=[cast_v] + gw("sqb"), inc=[Cc])
                pr.op("tensor", "matmul", ps_r1, onescol[:, :], sqb[:, :],
                      start=(ch == 0), stop=(ch == NCH - 1),
                      wait=[Cc.at()] + gw("ps_pt"), inc=[Tc])
                guard["sqb"] = (Tc, Tc.n)
            pr.op("vector", "tensor_copy", qrow[:, n * 512:(n + 1) * 512], ps_r1,
                  wait=[Tc.at()], inc=[V])
            guard["ps_pt"] = (V, V.n)
            pr.op("tensor", "matmul", ps_a[:, :], ones1[:, :],
                  srow[:, n * 512:(n + 1) * 512], start=True, stop=True,
                  wait=[V.at()] + gw("ps_a"), inc=[Tc])
            pr.op("tensor", "matmul", ps_b[:, :], ones1[:, :],
                  qrow[:, n * 512:(n + 1) * 512], start=True, stop=True,
                  wait=[V.at()] + gw("ps_b"), inc=[Tc])
            bc_v = Tc.at()
            pr.op("vector", "tensor_scalar_mul", muf[:, :], ps_a[:, :], i1024[:, :],
                  wait=[bc_v], inc=[V])
            pr.op("vector", "tensor_scalar_mul", sqf[:, :], ps_b[:, :], i1024[:, :],
                  inc=[V])
            guard["ps_a"] = (V, V.n)
            guard["ps_b"] = (V, V.n)
            pr.op("scalar", "activation", m2f[:, :], muf[:, :], AF.Square,
                  wait=[V.at()], inc=[Cc])
            pr.op("vector", "tensor_tensor", sqf[:, :], sqf[:, :], m2f[:, :],
                  ALU.subtract, wait=[Cc.at()], inc=[V])
            pr.op("scalar", "activation", m2f[:, :], sqf[:, :], AF.Sqrt,
                  bias=epscol[:, :], scale=1.0, wait=[V.at()], inc=[Cc])
            pr.op("vector", "reciprocal", RSTD[:, n * 512:(n + 1) * 512],
                  m2f[:, :], wait=[Cc.at()], inc=[V], lowp=True)
            pr.op("vector", "tensor_tensor", MURS[:, n * 512:(n + 1) * 512],
                  muf[:, :], RSTD[:, n * 512:(n + 1) * 512], ALU.mult, inc=[V])
        stats_v = V.at()
        for ch in range(NCH):
            msrc = cast_to[:, ch * cast_stride: ch * cast_stride + tl]
            pr.op("vector", "tensor_tensor", tmpf[:, 0:tl], msrc,
                  RSTD[:, 0:tl], ALU.mult, wait=[stats_v] + list(norm_wait), inc=[V])
            dst = out_to[:, ch * out_stride: ch * out_stride + tl]
            pr.op("vector", "tensor_tensor", dst, tmpf[:, 0:tl],
                  MURS[:, 0:tl], ALU.subtract, inc=[V])
            pr.op("vector", "tensor_scalar", dst, dst, gcol(ch), bcol(ch),
                  ALU.mult, ALU.add, inc=[V])
        return V.at()

    def mk_stats(scope, tag):
        A = lambda n, sh, dt=F32: scope.enter_context(nc.sbuf_tensor(n + tag, sh, dt))
        return dict(RSTD=A("RSTD", [128, T], BF16), MURS=A("MURS", [128, T], BF16),
                    srow=A("srow", [1, T], BF16), qrow=A("qrow", [1, T], BF16),
                    sqb=A("sqb", [128, 512], BF16), muf=A("muf", [128, 512]),
                    sqf=A("sqf", [128, 512]), m2f=A("m2f", [128, 512]),
                    tmpf=A("tmpf", [128, T]))

    x1_v = None
    spill_s = None
    for b in range(_RUN["batches"]):
        with contextlib.ExitStack() as scA:
            A = lambda n, sh, dt=BF16: scA.enter_context(
                nc.sbuf_tensor(f"{n}_b{b}", sh, dt))
            stA = mk_stats(scA, f"_A{b}")
            nA = A("nA", [128, NCH * T])
            wdqT_b = A("wdqT_b", [128, C]); wdkvT_b = A("wdkvT_b", [128, C])
            asbL = A("asbL", [128, NH * 128]); hsbL = A("hsbL", [128, C])
            ckvT = A("ckvT", [128, T]); ckvTM = A("ckvTM", [128, T])
            qlatT = A("qlatT", [128, T]); qlT = A("qlT", [128, T])
            paO = A("paO", [128, 3 * T]); ptG = A("ptG", [128, 16 * 512])
            ctxT = A("ctxT", [128, 2 * T])
            PG = pg()
            for ch in range(NCH if _RUN["loads"] else 0):
                pr.dma("sync", chunk(xb, ch), xT[b, ch], LS, wait=PG + gw("xb"))
            for ch in range(NCH if _RUN["loads"] else 0):
                pr.dma("gpsimd", wdqT_b[:, 128 * ch:128 * (ch + 1)], d_wdqT[ch],
                       LG, wait=PG)
                pr.dma("gpsimd", wdkvT_b[:, 128 * ch:128 * (ch + 1)], d_wdkvT[ch], LG)
            if _RUN["loads"] and _RUN["aw"]:
                pr.dma("gpsimd", asbL[:, :], asbscr[:, :], LG, wait=PG + [asb_st])
                pr.dma("gpsimd", hsbL[:, :], hsbscr[:, :], LG)
            x_l = LS.at()
            w_l = LG.at()

            if not _RUN["p1"]:
                continue
            n1_v = ln_block(stA, xb, False, nA, T, nA, T,
                            lambda ch: mcol(1, b, ch), lambda ch: mcol(0, b, ch),
                            0, T, wait0=[x_l, mod_ready] + PG)

            if not _RUN["p3"]:
                continue
            for n in range(4):
                for ch in range(NCH):
                    pr.op("tensor", "matmul", ps_a[:, :],
                          wdkvT_b[:, 128 * ch:128 * (ch + 1)],
                          chunk(nA, ch, n * 512, 512), start=(ch == 0),
                          stop=(ch == NCH - 1), wait=[n1_v, w_l] + gw("ps_a"), inc=[Tc])
                pr.op("vector", "tensor_copy", ckvT[:, n * 512:(n + 1) * 512],
                      ps_a[:, :], wait=[Tc.at()], inc=[V])
                guard["ps_a"] = (V, V.n)
                for ch in range(NCH):
                    pr.op("tensor", "matmul", ps_b[:, :],
                          wdqT_b[:, 128 * ch:128 * (ch + 1)],
                          chunk(nA, ch, n * 512, 512), start=(ch == 0),
                          stop=(ch == NCH - 1), wait=[n1_v, w_l] + gw("ps_b"), inc=[Tc])
                pr.op("vector", "tensor_copy", qlatT[:, n * 512:(n + 1) * 512],
                      ps_b[:, :], wait=[Tc.at()], inc=[V])
                guard["ps_b"] = (V, V.n)
            ckv_v = V.at()
            for j in range(16):
                pr.op("tensor", "matmul", ps_pt[:, 0:128],
                      ckvT[:, 128 * j:128 * (j + 1)], idn[:, :], start=True, stop=True,
                      wait=[ckv_v, consts_l[1]] + gw("ps_pt"), inc=[Tc])
                pr.op("vector", "tensor_copy", ckvTM[:, 128 * j:128 * (j + 1)],
                      ps_pt[:, 0:128], wait=[Tc.at()], inc=[V])
                guard["ps_pt"] = (V, V.n)
            ckvtm_v = V.at()
            if "ckvT" in dbg and b == 0:
                dbg_outs["ckvT"] = DO("dbg_ckvT", [128, T])
                pr.dma("gpsimd", dbg_outs["ckvT"][:, :], ckvT[:, :], SG, wait=[ckv_v])
                dbg_outs["qlatT"] = DO("dbg_qlatT", [128, T])
                pr.dma("gpsimd", dbg_outs["qlatT"][:, :], qlatT[:, :], SG, wait=[ckv_v])
                dbg_outs["n1"] = DO("dbg_n1", [128, NCH * T])
                guard["nAdbg"] = pr.dma("gpsimd", dbg_outs["n1"][:, :], nA[:, :], SG,
                                        wait=[n1_v])

            for h in range(NH if _RUN["p4"] else 0):
                for n in range(4):
                    pr.op("tensor", "matmul", ps_b[:, :], asbL[:, 128 * h:128 * (h + 1)],
                          qlatT[:, 512 * n:512 * (n + 1)], start=True, stop=True,
                          wait=[ckv_v, w_l] + gw("ps_b") + gw(("pt", 1)), inc=[Tc])
                    pr.op("vector", "tensor_copy", qlT[:, 512 * n:512 * (n + 1)],
                          ps_b[:, :], wait=[Tc.at()], inc=[V])
                    guard["ps_b"] = (V, V.n)
                ql_v = V.at()
                pending = []
                pt_state = {}
                ptGv = ptG[:, :].rearrange("p (j s) -> p j s", s=512)

                def flush_pt(prev):
                    ti, par, rc_c = prev
                    slen = 128 * (ti + 1)
                    pr.op("vector", "tensor_scalar",
                          paO[:, 2048 * par:2048 * par + slen],
                          paO[:, 2048 * par:2048 * par + slen],
                          statc[:, 8 * par + 6:8 * par + 7], None, ALU.mult,
                          wait=[(Cc, rc_c)], inc=[V])
                    norm_v = V.n
                    sub = ti % 4
                    for gi, j0 in enumerate(range(0, ti + 1, 4)):
                        jn = min(4, ti + 1 - j0)
                        gpar = gi % 2
                        bank = ps_pt if gpar == 0 else ps_b
                        for j in range(j0, j0 + jn):
                            pr.op("tensor", "matmul",
                                  bank[:, 128 * (j - j0):128 * (j - j0 + 1)],
                                  paO[:, 2048 * par + 128 * j:2048 * par + 128 * (j + 1)],
                                  idn[:, :], start=True, stop=True,
                                  wait=[(V.sem, norm_v)] + gw(("pt", gpar)) +
                                       (gw("ps_b") if gpar == 1 else []), inc=[Tc])
                        if gi % 2 == 0:
                            pr.op("vector", "tensor_copy",
                                  ptGv[:, j0:j0 + jn, 128 * sub:128 * (sub + 1)],
                                  bank[:, 0:128 * jn].rearrange("p (j s) -> p j s", s=128),
                                  wait=[Tc.at()], inc=[V])
                            guard[("pt", gpar)] = (V, V.n)
                        else:
                            pr.op("scalar", "copy",
                                  ptGv[:, j0:j0 + jn, 128 * sub:128 * (sub + 1)],
                                  bank[:, 0:128 * jn].rearrange("p (j s) -> p j s", s=128),
                                  wait=[Tc.at()], inc=[Cc])
                            guard[("pt", gpar)] = (Cc, Cc.n)
                    pt_state[par] = (Tc, Tc.n)

                def flush_pv(g):
                    pt_c = V.at()
                    pt_c2 = Cc.at()
                    tmax = 4 * g + 3
                    for j in range(tmax + 1):
                        lo = max(0, j - 4 * g)
                        nvalid = 4 - lo
                        pr.op("tensor", "matmul",
                              ps_cx[:, 128 * lo:128 * (lo + nvalid)],
                              ckvTM[:, 128 * j:128 * (j + 1)],
                              ptG[:, 512 * j + 128 * lo:512 * j + 128 * (lo + nvalid)],
                              start=(j == 0), stop=(j == tmax),
                              wait=[pt_c, pt_c2, ckvtm_v] + gw("ps_cx"), inc=[Tc])
                    pr.op("scalar", "copy",
                          ctxT[:, (h % 2) * T + 512 * g:(h % 2) * T + 512 * (g + 1)],
                          ps_cx[:, :], wait=[Tc.at()], inc=[Cc])
                    guard["ps_cx"] = (Cc, Cc.n)

                for ti in range(16):
                    par = ti % 3
                    slen = 128 * (ti + 1)
                    nmm = (slen + 511) // 512
                    for n in range(nmm):
                        wd = min(512, slen - 512 * n)
                        pr.op("tensor", "matmul", ps_sc[:, 512 * n:512 * n + wd],
                              qlT[:, 128 * ti:128 * (ti + 1)],
                              ckvT[:, 512 * n:512 * n + wd], start=True,
                              stop=(n != nmm - 1),
                              wait=[ql_v] + gw("ps_sc") + gw(("sc", 0)) + gw(("sc", 1)),
                              inc=[Tc])
                        if n == nmm - 1:
                            pr.op("tensor", "matmul", ps_sc[:, slen - 128:slen],
                                  idn[:, :], maskc_b[:, :], start=False, stop=True,
                                  wait=[consts_l[1]], inc=[Tc])
                        pr.op("vector", "tensor_reduce",
                              statc[:, 8 * par + n:8 * par + n + 1],
                              ps_sc[:, 512 * n:512 * n + wd], axis=AXX, op=ALU.max,
                              wait=[Tc.at()] + gw(("st", par)), inc=[V])
                    pr.bar("vector")
                    pr.op("vector", "tensor_reduce", statc[:, 8 * par + 4:8 * par + 5],
                          statc[:, 8 * par:8 * par + nmm], axis=AXX, op=ALU.max,
                          negate=True, inc=[V])
                    mask_v = V.at()
                    pr.op("scalar", "activation",
                          paO[:, 2048 * par:2048 * par + slen], ps_sc[:, 0:slen],
                          AF.Exp, bias=statc[:, 8 * par + 4:8 * par + 5],
                          scale=onecol_f[:, :],
                          accum_out=statc[:, 8 * par + 5:8 * par + 6],
                          wait=[mask_v] + ([pt_state[par]] if par in pt_state else []),
                          inc=[Cc])
                    guard["ps_sc"] = (Cc, Cc.n)
                    guard[("st", par)] = (Cc, Cc.n)
                    pr.op("scalar", "activation", statc[:, 8 * par + 7:8 * par + 8],
                          statc[:, 8 * par + 5:8 * par + 6], AF.Ln, inc=[Cc])
                    pr.op("scalar", "activation", statc[:, 8 * par + 6:8 * par + 7],
                          statc[:, 8 * par + 7:8 * par + 8], AF.Exp,
                          scale=negone[:, :], inc=[Cc])
                    pending.append((ti, par, Cc.n))
                    if len(pending) > 2:
                        p_ = pending.pop(0)
                        flush_pt(p_)
                        if p_[0] % 4 == 3:
                            flush_pv(p_[0] // 4)
                for p_ in pending:
                    flush_pt(p_)
                    if p_[0] % 4 == 3:
                        flush_pv(p_[0] // 4)
                if h % 2 == 1:
                    jj = h // 2
                    ctx_v = Cc.at()
                    for n in range(4):
                        pr.op("tensor", "matmul", ps_a[0:64, :],
                              hsbL[:, 128 * jj:128 * jj + 64],
                              ctxT[:, 512 * n:512 * (n + 1)], start=True, stop=True,
                              tile_position=(0, 0),
                              wait=[ctx_v, w_l] + gw("ps_a"), inc=[Tc])
                        pr.op("tensor", "matmul", ps_a[64:128, :],
                              hsbL[:, 128 * jj + 64:128 * (jj + 1)],
                              ctxT[:, T + 512 * n:T + 512 * (n + 1)], start=True,
                              stop=True, tile_position=(0, 64), inc=[Tc])
                        pr.op("vector", "scalar_tensor_tensor",
                              chunk(xb, jj, 512 * n, 512), ps_a[:, :], mcol(2, b, jj),
                              chunk(xb, jj, 512 * n, 512), ALU.mult, ALU.add,
                              wait=[Tc.at()], inc=[V])
                        guard["ps_a"] = (V, V.n)
            attn_v = V.at()
            if "x_attn" in dbg and b == 0:
                dbg_outs["x_attn"] = DO("dbg_x_attn", [128, NCH * T])
                guard["xbdbg"] = pr.dma("sync", dbg_outs["x_attn"][:, :], xb[:, :], SO,
                                        wait=[attn_v])

            for ch in range(NCH):
                pr.dma("sync", spill[ch], chunk(xb, ch),
                       SO, wait=[attn_v] + gw("spill"))
            spill_s = SO.at()
            x1_v = ln_block(stA, xb, False, nA, T, xb, T,
                            lambda ch: mcol(4, b, ch), lambda ch: mcol(3, b, ch),
                            0, T, wait0=[(Tc, Tc.n)] + gw("nAdbg"),
                            norm_wait=[spill_s] + gw("xbdbg"))
            if "x1h" in dbg and b == 0:
                dbg_outs["x1h"] = DO("dbg_x1h", [128, NCH * T])
                pr.dma("sync", dbg_outs["x1h"][:, :], xb[:, :], SO, wait=[x1_v])

        for L in range(_RUN["layers"]):
            with contextlib.ExitStack() as scB:
                A = lambda n, sh, dt=BF16: scB.enter_context(
                    nc.sbuf_tensor(f"{n}_b{b}L{L}", sh, dt))
                tfold = A("tfold", [128, 4 * T])
                vtm = A("vtm", [128, 64 * 128])
                nAm = A("nAm", [128, NCH * T])
                nB = A("nB", [128, 4 * T])
                nC = A("nC", [128, 4 * T])
                PG = pg()
                with contextlib.ExitStack() as scT:
                    thl = scT.enter_context(
                        nc.sbuf_tensor(f"thl_b{b}L{L}", [128, 2 * T], F32))
                    for tb in range(4):
                        pr.dma("sync", thl[:, 0:T], d_tabs[tb, 0],
                               LS, wait=PG + gw("thl"))
                        pr.dma("sync", thl[:, T:2 * T], d_tabs[tb, 1], LS,
                               wait=PG + gw("thl"))
                        pr.op("vector", "tensor_tensor", thl[:, 0:T], thl[:, 0:T],
                              thl[:, T:2 * T], ALU.add, wait=[LS.at()] + PG, inc=[V])
                        pr.op("vector", "tensor_scalar_mul",
                              tfold[:, tb * T:(tb + 1) * T], thl[:, 0:T],
                              qscol[:, 4 * L + tb:4 * L + tb + 1],
                              wait=[consts_l[0]], inc=[V])
                        guard["thl"] = (V, V.n)
                fold_v = V.at()
                for ch in range(NCH):
                    dst = chunk(nAm, ch)
                    if ch >= 4:
                        pr.op("vector", "memset", dst[:, 0:1], 0.0,
                              wait=[x1_v] + PG, inc=[V])
                        pr.op("vector", "tensor_copy", dst[:, 1:T],
                              xb[:, ch * T: ch * T + T - 1], inc=[V])
                    else:
                        pr.op("vector", "tensor_copy", dst, chunk(xb, ch),
                              wait=[x1_v] + PG, inc=[V])
                mb_v = V.at()
                with contextlib.ExitStack() as scQ:
                    AQ = lambda n, sh, dt=BF16: scQ.enter_context(
                        nc.sbuf_tensor(f"{n}_b{b}L{L}", sh, dt))
                    segw = AQ("segw", [128, NCH * 512])
                    PGQ = pg()
                    sqbA = AQ("sqbA", [128, T])
                    ssrow = AQ("ssrow", [2, T])
                    rotb = AQ("rotb", [128, T])
                    tmpa = AQ("tmpa", [128, T])
                    rbuf = AQ("rbuf", [128, T])
                    m2fA = AQ("m2fA", [128, 512], F32)
                    for pss in range(3):
                        for k in range(NCH):
                            pr.dma("gpsimd", segw[:, 512 * k:512 * (k + 1)],
                                   d_wqkv3[L, pss, k],
                                   LG, wait=PGQ + gw(("segw", k)))
                        seg_l = LG.at()
                        for oc in range(4):
                            for n in range(4):
                                for k in range(NCH):
                                    pr.op("tensor", "matmul", ps_a[:, :],
                                          segw[:, 512 * k + 128 * oc:512 * k + 128 * (oc + 1)],
                                          chunk(nAm, k, 512 * n, 512),
                                          start=(k == 0), stop=(k == NCH - 1),
                                          wait=[seg_l, mb_v] + gw("ps_a"), inc=[Tc])
                                mm_t = Tc.at()
                                if pss == 0:
                                    pr.op("vector", "tensor_copy",
                                          nB[:, oc * T + 512 * n:oc * T + 512 * (n + 1)],
                                          ps_a[:, :], wait=[mm_t], inc=[V])
                                    guard["ps_a"] = (V, V.n)
                                elif pss == 1:
                                    pr.op("vector", "tensor_copy",
                                          nC[:, oc * T + 512 * n:oc * T + 512 * (n + 1)],
                                          ps_a[:, :], wait=[mm_t], inc=[V])
                                    guard["ps_a"] = (V, V.n)
                                else:
                                    pr.op("vector", "tensor_copy", tmpa[:, 0:512],
                                          ps_a[:, :], wait=[mm_t] + gw("tmpa"), inc=[V])
                                    guard["ps_a"] = (V, V.n)
                                    for j in range(4):
                                        tj = 4 * n + j
                                        pr.op("tensor", "matmul", ps_pt[:, 0:128],
                                              tmpa[:, 128 * j:128 * (j + 1)], idn[:, :],
                                              start=True, stop=True,
                                              wait=[V.at()] + gw("ps_pt"), inc=[Tc])
                                        pr.op("vector", "tensor_copy",
                                              vtm[:, (4 * tj + oc) * 128:(4 * tj + oc + 1) * 128],
                                              ps_pt[:, 0:128], wait=[Tc.at()], inc=[V])
                                        guard["ps_pt"] = (V, V.n)
                                    guard["tmpa"] = (Tc, Tc.n)
                        for k in range(NCH):
                            guard[("segw", k)] = (Tc, Tc.n)
                    qkv_v = V.at()
                    for side, buf in ((0, nB), (1, nC)):
                        for cc in range(4):
                            src = buf[:, cc * T:(cc + 1) * T]
                            pr.op("scalar", "activation", sqbA[:, :], src, AF.Square,
                                  wait=[qkv_v] + gw("sqbA"), inc=[Cc])
                            for n in range(4):
                                pr.op("tensor", "matmul", ps_s2, e2[:, :],
                                      sqbA[:, 512 * n:512 * (n + 1)], start=True,
                                      stop=True,
                                      wait=[Cc.at(), consts_l[1]] + gw("ps_pt"),
                                      inc=[Tc])
                                pr.op("vector", "tensor_copy",
                                      ssrow[:, 512 * n:512 * (n + 1)], ps_s2,
                                      wait=[Tc.at()], inc=[V])
                                guard["ps_pt"] = (V, V.n)
                            guard["sqbA"] = (Tc, Tc.n)
                            for n in range(4):
                                pr.op("tensor", "matmul", ps_b[:, :], e2t[:, :],
                                      ssrow[:, 512 * n:512 * (n + 1)], start=True,
                                      stop=True,
                                      wait=[V.at()] + gw("ps_b") + gw(("pt", 1)),
                                      inc=[Tc])
                                sc = i64[:, :] if side == 0 else onecol_f[:, :]
                                pr.op("scalar", "activation", m2fA[:, :], ps_b[:, :],
                                      AF.Sqrt, scale=sc, wait=[Tc.at(), V.at()],
                                      inc=[Cc])
                                guard["ps_b"] = (Cc, Cc.n)
                                clipv = 1.25e-13 if side == 0 else 1.0e-12
                                pr.op("vector", "tensor_scalar", m2fA[:, :], m2fA[:, :],
                                      clipv, None, ALU.max, wait=[Cc.at()], inc=[V])
                                pr.op("vector", "reciprocal",
                                      rbuf[:, 512 * n:512 * (n + 1)], m2fA[:, :],
                                      inc=[V], lowp=True)
                            for g0, g1 in ((0, 32), (32, 0), (64, 96), (96, 64)):
                                pr.op("vector", "tensor_copy", rotb[g0:g0 + 32, :],
                                      buf[g1:g1 + 32, cc * T:(cc + 1) * T], inc=[V])
                            tb0 = (2 * side) * T
                            tb1 = (2 * side + 1) * T
                            pr.op("vector", "tensor_tensor", tmpa[:, :], src,
                                  tfold[:, tb0:tb0 + T], ALU.mult,
                                  wait=[fold_v, (Tc, Tc.n)], inc=[V])
                            pr.op("vector", "tensor_tensor", rotb[:, :], rotb[:, :],
                                  tfold[:, tb1:tb1 + T], ALU.mult, inc=[V])
                            pr.op("vector", "tensor_tensor", tmpa[:, :], tmpa[:, :],
                                  rotb[:, :], ALU.add, inc=[V])
                            pr.op("vector", "tensor_tensor", src, tmpa[:, :],
                                  rbuf[:, :], ALU.mult, inc=[V])
                            guard["tmpa"] = (V, V.n)
                    rot_v = V.at()
                if not _RUN["attn"]:
                    continue
                with contextlib.ExitStack() as scL:
                    AL = lambda n, sh, dt=BF16: scL.enter_context(
                        nc.sbuf_tensor(f"{n}_b{b}L{L}", sh, dt))
                    pa4 = AL("pa4", [128, 6 * 1024])
                    pt2 = AL("pt2", [128, 2 * 1024])
                    oTb = AL("oTb", [128, 4 * T])
                    wosb = AL("wosb", [128, 4 * C])
                    pending = []
                    pt_state = {}

                    def flush_prev(prev):
                        cc, w, qi, klo, klen, par, sum_c = prev
                        pr.op("scalar", "activation",
                              statc[:, 8 * par + 4:8 * par + 6],
                              statc[:, 8 * par + 6:8 * par + 8], AF.Ln,
                              wait=[(Cc, sum_c)], inc=[Cc])
                        pr.op("scalar", "activation",
                              statc[:, 8 * par + 4:8 * par + 6],
                              statc[:, 8 * par + 4:8 * par + 6], AF.Exp,
                              scale=negone[:, :], inc=[Cc])
                        rc_c = Cc.at()
                        for hh in range(2):
                            pr.op("vector", "tensor_scalar",
                                  pa4[:, (2 * par + hh) * 1024:
                                      (2 * par + hh) * 1024 + klen],
                                  pa4[:, (2 * par + hh) * 1024:
                                      (2 * par + hh) * 1024 + klen],
                                  statc[:, 8 * par + 4 + hh:8 * par + 5 + hh],
                                  None, ALU.mult, wait=[rc_c], inc=[V])
                        norm_v = V.n
                        nk = klen // 128
                        gpar = par % 2
                        bank = ps_pt if gpar == 0 else ps_b
                        gi = 0
                        for hh in range(2):
                            for j0 in range(0, nk, 4):
                                jn = min(4, nk - j0)
                                for j in range(j0, j0 + jn):
                                    pr.op("tensor", "matmul",
                                          bank[:, 128 * (j - j0):128 * (j - j0 + 1)],
                                          pa4[:, (2 * par + hh) * 1024 + 128 * j:
                                              (2 * par + hh) * 1024 + 128 * (j + 1)],
                                          idn[:, :], start=True, stop=True,
                                          wait=[(V.sem, norm_v)] + gw(("pt", gpar)) +
                                               (gw("ps_b") if gpar == 1 else []),
                                          inc=[Tc])
                                if gi % 2 == 0:
                                    pr.op("vector", "tensor_copy",
                                          pt2[:, 1024 * hh + 128 * j0:
                                              1024 * hh + 128 * (j0 + jn)],
                                          bank[:, 0:128 * jn], wait=[Tc.at()], inc=[V])
                                    guard[("pt", gpar)] = (V, V.n)
                                else:
                                    pr.op("scalar", "copy",
                                          pt2[:, 1024 * hh + 128 * j0:
                                              1024 * hh + 128 * (j0 + jn)],
                                          bank[:, 0:128 * jn], wait=[Tc.at()], inc=[Cc])
                                    guard[("pt", gpar)] = (Cc, Cc.n)
                                gi += 1
                        pt_c = V.at()
                        pt_c2 = Cc.at()
                        for hh in range(2):
                            for j in range(nk):
                                tj = (klo + 128 * j) // 128
                                pr.op("tensor", "matmul",
                                      ps_cx[64 * hh:64 * hh + 64, 0:128],
                                      vtm[:, (4 * tj + cc) * 128 + 64 * hh:
                                          (4 * tj + cc) * 128 + 64 * hh + 64],
                                      pt2[:, 1024 * hh + 128 * j:
                                          1024 * hh + 128 * (j + 1)],
                                      start=(j == 0), stop=(j == nk - 1),
                                      tile_position=(0, 64 * hh),
                                      wait=[pt_c, pt_c2] + gw("ps_cx"), inc=[Tc])
                        pr.op("scalar", "copy",
                              oTb[:, cc * T + 512 * w + 128 * qi:
                                  cc * T + 512 * w + 128 * (qi + 1)],
                              ps_cx[:, 0:128], wait=[Tc.at()], inc=[Cc])
                        guard["ps_cx"] = (Cc, Cc.n)
                        pt_state[par] = (Tc, Tc.n)

                    it = 0
                    for cc in range(4):
                        for w in range(4):
                            for qi in range(4):
                                if w == 0:
                                    klo, klen, amask = 0, 128 * (qi + 1), False
                                else:
                                    klo, klen, amask = 512 * (w - 1) + 128 * qi, 640, True
                                q0 = cc * T + 512 * w + 128 * qi
                                par = it % 3
                                it += 1
                                for hh in range(2):
                                    hl = 64 * hh
                                    pst = ps_sc[:, 1024 * hh:1024 * hh + klen]
                                    w1sz = min(512, klen)
                                    pr.op("tensor", "matmul", pst[:, 0:w1sz],
                                          nB[hl:hl + 64, q0:q0 + 128],
                                          nC[hl:hl + 64, cc * T + klo:
                                             cc * T + klo + w1sz],
                                          start=True, stop=False,
                                          tile_position=(hl, 0),
                                          wait=[rot_v] + gw(("sc", hh)) + gw("ps_sc"),
                                          inc=[Tc])
                                    if amask:
                                        pr.op("tensor", "matmul", pst[:, 0:128],
                                              idn[:, :], maska_b[:, :],
                                              start=False, stop=True,
                                              wait=[consts_l[1]], inc=[Tc])
                                        pr.op("tensor", "matmul", pst[:, 512:640],
                                              nB[hl:hl + 64, q0:q0 + 128],
                                              nC[hl:hl + 64, cc * T + klo + 512:
                                                 cc * T + klo + 640],
                                              start=True, stop=False,
                                              tile_position=(hl, 0), inc=[Tc])
                                    pr.op("tensor", "matmul",
                                          pst[:, klen - 128:klen],
                                          idn[:, :], maskc_b[:, :],
                                          start=False, stop=True,
                                          wait=[consts_l[1]], inc=[Tc])
                                qk_t = Tc.at()
                                for hh in range(2):
                                    pr.op("vector", "tensor_reduce",
                                          statc[:, 8 * par + hh:8 * par + hh + 1],
                                          ps_sc[:, 1024 * hh:1024 * hh + klen],
                                          axis=AXX, op=ALU.max, negate=True,
                                          wait=[qk_t] + gw(("st", par)), inc=[V])
                                mask_v = V.at()
                                for hh in range(2):
                                    pr.op("scalar", "activation",
                                          pa4[:, (2 * par + hh) * 1024:
                                              (2 * par + hh) * 1024 + klen],
                                          ps_sc[:, 1024 * hh:1024 * hh + klen],
                                          AF.Exp,
                                          bias=statc[:, 8 * par + hh:8 * par + hh + 1],
                                          scale=onecol_f[:, :],
                                          accum_out=statc[:, 8 * par + 6 + hh:
                                                          8 * par + 7 + hh],
                                          wait=[mask_v] + ([pt_state[par]]
                                                           if par in pt_state else []),
                                          inc=[Cc])
                                guard["ps_sc"] = (Cc, Cc.n)
                                guard[("st", par)] = (Cc, Cc.n)
                                pending.append((cc, w, qi, klo, klen, par, Cc.n))
                                if len(pending) > 2:
                                    flush_prev(pending.pop(0))
                    for p_ in pending:
                        flush_prev(p_)
                    loc_v = Cc.at()
                    for k in range(4):
                        pr.dma("gpsimd", wosb[:, C * k:C * (k + 1)], d_wo4[L, k],
                               LG, wait=pg())
                    wo_l = LG.at()
                    for oc in range(NCH):
                        for n in range(4):
                            for k in range(4):
                                pr.op("tensor", "matmul", ps_a[:, :],
                                      wosb[:, C * k + 128 * oc:C * k + 128 * (oc + 1)],
                                      oTb[:, k * T + 512 * n:k * T + 512 * (n + 1)],
                                      start=(k == 0), stop=(k == 3),
                                      wait=[wo_l, loc_v] + gw("ps_a"), inc=[Tc])
                            pr.op("vector", "scalar_tensor_tensor",
                                  chunk(xb, oc, 512 * n, 512), ps_a[:, :],
                                  onecol_f[:, :], chunk(xb, oc, 512 * n, 512),
                                  ALU.mult, ALU.add, wait=[Tc.at()], inc=[V])
                            guard["ps_a"] = (V, V.n)
                    x1a_v = V.at()
                    if "x1a" in dbg and b == 0 and L == 0:
                        dbg_outs["x1a"] = DO("dbg_x1a", [128, NCH * T])
                        pr.dma("sync", dbg_outs["x1a"][:, :], xb[:, :], SO,
                               wait=[x1a_v])
                    if "x1a1" in dbg and b == 0 and L == 1:
                        dbg_outs["x1a1"] = DO("dbg_x1a1", [128, NCH * T])
                        pr.dma("sync", dbg_outs["x1a1"][:, :], xb[:, :], SO,
                               wait=[x1a_v])

            if not _RUN["mlp"]:
                continue
            with contextlib.ExitStack() as scM:
                AM = lambda n, sh, dt=BF16: scM.enter_context(
                    nc.sbuf_tensor(f"{n}_b{b}L{L}", sh, dt))
                stM = mk_stats(scM, f"_M{b}{L}")
                mb2 = AM("mb2", [128, NCH * 1024])
                nn2 = AM("nn2", [128, NCH * 1024])
                h1b = AM("h1b", [128, NJ * 1024])
                w1s = [AM(f"w1s{i}", [128, 2 * 1024]) for i in range(2)]
                w2s = [AM(f"w2s{i}", [128, NJ * 128]) for i in range(2)]
                gsc = AM("gsc", [128, 512])
                sv = scM.enter_context(nc.sbuf_tensor(f"sv_b{b}L{L}", [128, 4], F32))
                pr.op("vector", "tensor_copy",
                      sv[:, 0:4].rearrange("p (c o) -> p c o", o=1),
                      xb[:, :].rearrange("p (c t) -> p c t", t=T)[:, 4:8, 1023:1024],
                      wait=pg(), inc=[V])
                pr.bar("vector")
                for half in range(2):
                    t0 = 1024 * half
                    nh_v = ln_block(stM, xb, True, mb2, 1024, nn2, 1024,
                                    lambda ch: lgcS[:, NCH * L + ch:NCH * L + ch + 1],
                                    lambda ch: lbcS[:, NCH * L + ch:NCH * L + ch + 1],
                                    t0, 1024, wait0=pg(),
                                    edge=(lambda ch: sv[:, ch - 4:ch - 3])
                                         if half == 1 else None)
                    for j in range(NJ):
                        par = j % 2
                        pr.dma("gpsimd", w1s[par][:, 0:1024], d_w1a[L, j], LG,
                               wait=(pg() if (j < 2 and half == 0) else
                                     gw(("w1", par))))
                        pr.dma("gpsimd", w1s[par][:, 1024:2048], d_w1g[L, j], LG)
                        w1_l = LG.at()
                        for n2 in range(2):
                            for k in range(NCH):
                                pr.op("tensor", "matmul", ps_a[:, :],
                                      w1s[par][:, 128 * k:128 * (k + 1)],
                                      nn2[:, k * 1024 + 512 * n2:k * 1024 + 512 * (n2 + 1)],
                                      start=(k == 0), stop=(k == NCH - 1),
                                      wait=[w1_l, nh_v] + gw("ps_a"), inc=[Tc])
                            for k in range(NCH):
                                pr.op("tensor", "matmul", ps_b[:, :],
                                      w1s[par][:, 1024 + 128 * k:1024 + 128 * (k + 1)],
                                      nn2[:, k * 1024 + 512 * n2:k * 1024 + 512 * (n2 + 1)],
                                      start=(k == 0), stop=(k == NCH - 1),
                                      wait=[w1_l] + gw("ps_b"), inc=[Tc])
                            mm_t = Tc.at()
                            pr.op("scalar", "activation", gsc[:, :], ps_b[:, :],
                                  AF.Gelu, bias=b1gcS[:, NJ * L + j:NJ * L + j + 1],
                                  scale=1.0, wait=[mm_t] + gw("gsc"), inc=[Cc])
                            guard["ps_b"] = (Cc, Cc.n)
                            pr.op("vector", "scalar_tensor_tensor",
                                  h1b[:, 1024 * j + 512 * n2:1024 * j + 512 * (n2 + 1)],
                                  ps_a[:, :], b1acS[:, NJ * L + j:NJ * L + j + 1],
                                  gsc[:, :], ALU.add, ALU.mult,
                                  wait=[Cc.at()], inc=[V])
                            guard["ps_a"] = (V, V.n)
                            guard["gsc"] = (V, V.n)
                        guard[("w1", par)] = (Tc, Tc.n)
                    h1_v = V.at()
                    for oc in range(NCH):
                        par = oc % 2
                        pr.dma("gpsimd", w2s[par][:, :], d_w2b[L, oc], LG,
                               wait=(pg() if (oc < 2 and half == 0) else
                                     gw(("w2", par))))
                        w2_l = LG.at()
                        for n2 in range(2):
                            for k in range(NJ):
                                pr.op("tensor", "matmul", ps_a[:, :],
                                      w2s[par][:, 128 * k:128 * (k + 1)],
                                      h1b[:, 1024 * k + 512 * n2:1024 * k + 512 * (n2 + 1)],
                                      start=(k == 0), stop=(k == NJ - 1),
                                      wait=[w2_l, h1_v] + gw("ps_a"), inc=[Tc])
                            pr.op("vector", "scalar_tensor_tensor",
                                  chunk(xb, oc, t0 + 512 * n2, 512), ps_a[:, :],
                                  b2cS[:, NCH * L + oc:NCH * L + oc + 1],
                                  chunk(xb, oc, t0 + 512 * n2, 512), ALU.add, ALU.add,
                                  wait=[Tc.at()], inc=[V])
                            guard["ps_a"] = (V, V.n)
                        guard[("w2", par)] = (Tc, Tc.n)
                x1_v = V.at()
                if "x1L" in dbg and b == 0 and L == 0:
                    dbg_outs["x1L"] = DO("dbg_x1L", [128, NCH * T])
                    pr.dma("sync", dbg_outs["x1L"][:, :], xb[:, :], SO, wait=[x1_v])
                if "x1L1" in dbg and b == 0 and L == 1:
                    dbg_outs["x1L1"] = DO("dbg_x1L1", [128, NCH * T])
                    pr.dma("sync", dbg_outs["x1L1"][:, :], xb[:, :], SO, wait=[x1_v])

        if not _RUN["final"]:
            continue
        with contextlib.ExitStack() as scF:
            AFs = lambda n, sh: scF.enter_context(
                nc.sbuf_tensor(f"{n}_b{b}", sh, F32))
            xspb = [AFs(f"xsp{i}", [128, T]) for i in range(2)]
            obufb = [AFs(f"obuf{i}", [128, T]) for i in range(2)]
            PG = pg()
            for ch in range(NCH):
                par = ch % 2
                pr.dma("sync", xspb[par][:, :], spill[ch],
                       LS, wait=PG + [spill_s] + gw(("fin", par)))
                pr.op("vector", "scalar_tensor_tensor", obufb[par][:, :],
                      chunk(xb, ch), mcol(5, b, ch), xspb[par][:, :],
                      ALU.mult, ALU.add, wait=[LS.at(), x1_v], inc=[V])
                pr.dma("sync", outT[b, ch], obufb[par][:, :], SO, wait=[V.at()])
                guard[("fin", par)] = SO.at()
        guard["xb"] = SO.at()
        guard["spill"] = LS.at()

    pr.emit("sync", lambda e: e.nop(), wait=[SO.at(), SG.at()])

    with nc.Block() as block:
        pr.replay(block)
    es.close()
    return nc, dbg_outs


_DBG = ()


def kernel(**inputs):
    in_maps = _stage(inputs)
    nc, _ = build_program(_DBG)
    res = run_bass_kernel_spmd(nc, in_maps, core_ids=list(range(NCORES)))
    out = np.empty((16, T, C), np.float32)
    for core in range(NCORES):
        o = np.asarray(res.results[core]["outT"])
        out[core * BPC:(core + 1) * BPC] = (
            o.reshape(BPC, C, T).transpose(0, 2, 1))
    return out
